# version 04
# baseline (speedup 1.0000x reference)
import contextlib
import numpy as np
import concourse.bass as bass
import concourse.mybir as mybir
from concourse.bass_utils import run_bass_kernel_spmd
from concourse.alu_op_type import AluOpType as OP

F32 = mybir.dt.float32
BF16 = mybir.dt.bfloat16
I32 = mybir.dt.int32
AF = mybir.ActivationFunctionType

NCORE = 8
D = 1024
G = 8
E = 32
COREB = 4
T = 2048
TOK = COREB * T
CH = 512
NCH = TOK // CH
TPC = CH // 128
NTILE = TOK // 128
CL = 384
DUMP = E * CL
STILE = CL // 128
EPS = 1e-5

ENG = ("sync", "tensor", "vector", "scalar", "gpsimd")


class SemC:
    __slots__ = ("h", "n", "name")

    def __init__(self, h, name):
        self.h = h
        self.n = 0
        self.name = name


class Prog:
    CHAIN = ("vector", "scalar", "gpsimd")

    def __init__(self, nc, ctx):
        self.nc = nc
        self.ctx = ctx
        self.q = {e: [] for e in ENG}
        self.marks = {}
        self.selfsem = {}

    def init_selfsems(self):
        for e in self.CHAIN + ("tensor",):
            self.selfsem[e] = self.sem("self_" + e)

    def sem(self, name):
        return SemC(self.ctx.enter_context(self.nc.semaphore(name)), name)

    def do(self, eng, fn, mark=None, inc=None, n=1):
        if eng in self.CHAIN:
            s = self.selfsem[eng]
            if mark is not None:
                s.n += 1
                self.q[eng].append(lambda q, fn=fn, h=s.h: fn(q).then_inc(h, 1))
                self.marks[mark] = (s, s.n)
            else:
                self.q[eng].append(fn)
            self.q[eng].append(lambda q: q.drain())
        elif eng == "tensor":
            s = self.selfsem[eng]
            if mark is not None:
                s.n += 1
                self.q[eng].append(lambda q, fn=fn, h=s.h: fn(q).then_inc(h, 1))
                self.marks[mark] = (s, s.n)
            else:
                self.q[eng].append(fn)
        else:
            raise ValueError(eng)

    def dma(self, eng, fn, sem, mark=None):
        sem.n += 16
        self.q[eng].append(lambda q, fn=fn, s=sem.h: fn(q).then_inc(s, 16))
        if mark is not None:
            self.marks[mark] = (sem, sem.n)

    def wt(self, eng, sem, val=None):
        v = sem.n if val is None else val
        if v > 0:
            self.q[eng].append(lambda q, s=sem.h, v=v: q.wait_ge(s, v))

    def mark(self, key, x):
        s = self.selfsem[x] if isinstance(x, str) else x
        self.marks[key] = (s, s.n)

    def wtm(self, eng, key, required=True):
        if key not in self.marks:
            assert not required, f"missing mark {key}"
            return
        s, v = self.marks[key]
        self.wt(eng, s, v)

    def join(self, eng, sems):
        for s in sems:
            self.wt(eng, s, s.n)

    def finalize(self, block):
        for e in ENG:
            ops = self.q[e]

            def body(q, ops=ops):
                for f in ops:
                    f(q)

            getattr(block, e)(body)


def build_program(debug=False):
    nc = bass.Bass("TRN2", target_bir_lowering=False)
    dram = {}

    def din(name, shape, dt=F32):
        dram[name] = nc.dram_tensor(name, shape, dt, kind="ExternalInput").ap()
        return dram[name]

    def dout(name, shape, dt=F32):
        dram[name] = nc.dram_tensor(name, shape, dt, kind="ExternalOutput").ap()
        return dram[name]

    def dint(name, shape, dt=F32):
        dram[name] = nc.dram_tensor(name, shape, dt, kind="Internal").ap()
        return dram[name]

    xf = din("xf", [COREB, D, T])
    xt = din("xt", [TOK, D])
    wkT = din("wkT", [D, D])
    wvT = din("wvT", [D, D])
    wrT = din("wrT", [D, D])
    woT = din("woT", [D, D])
    wo_n = din("wo_n", [D, D])
    wrout = din("wrout", [D, E])
    wg = din("wg", [E, D, D])
    wu = din("wu", [E, D, D])
    wd = din("wd", [E, D, D])
    tmk = din("tmk", [128, G])
    tmv = din("tmv", [128, G])
    tmr = din("tmr", [128, G])
    uu = din("uu", [128, G])
    ww = din("ww", [128, G])
    iotab = din("iotab", [128, E])
    identf = din("identf", [128, 128])
    trib = din("trib", [128, 128])
    rstmk = din("rstmk", [128, TPC * E])

    out = dout("out", [TOK, D])
    if debug:
        dout("dbg_x1", [TOK, D])
        dout("dbg_route", [NTILE * 128, 4])
        dout("dbg_cnt", [1, E])

    wkbf = dint("wkbf", [D, D], BF16)
    wvbf = dint("wvbf", [D, D], BF16)
    wrbf = dint("wrbf", [D, D], BF16)
    bufq = dint("bufq", [DUMP + 1, D], BF16)
    ybuf = dint("ybuf", [DUMP + 1, D])
    x1sp = dint("x1sp", [TOK, D])

    octx = contextlib.ExitStack()
    P = Prog(nc, octx)
    marks = P.marks
    osb = lambda n, s, d=F32: octx.enter_context(nc.sbuf_tensor(n, s, d))
    ops_ = lambda n, s, d=F32: octx.enter_context(nc.psum_tensor(n, s, d))

    tmk_s = osb("tmk_s", [128, G])
    tmv_s = osb("tmv_s", [128, G])
    tmr_s = osb("tmr_s", [128, G])
    uu_s = osb("uu_s", [128, G])
    ww_s = osb("ww_s", [128, G])
    ewn_s = osb("ewn_s", [128, G])
    iota_s = osb("iota_s", [128, E])
    ident_s = osb("ident_s", [128, 128])
    tri_s = osb("tri_s", [128, 128], BF16)
    rstm_s = osb("rstm_s", [128, TPC, E])
    onescol_bf = osb("onescol_bf", [128, 1], BF16)
    onesrow_s = osb("onesrow_s", [1, 128])
    onecol_s = osb("onecol_s", [128, 1])
    epscol_s = osb("epscol_s", [128, 1])
    cs_s = osb("cs_s", [128, E])
    wr_s = osb("wr_s", [128, G, E], BF16)
    m_s = osb("m_s", [128, G, E], BF16)
    wo_s = osb("wo_s", [128, G, D], BF16)
    runb_s = osb("runb_s", [1, E])
    basesrow_s = osb("basesrow_s", [1, TPC * E])
    eid_s = osb("eid_s", [128, NTILE])
    lrank_s = osb("lrank_s", [128, NTILE])
    gate_s = osb("gate_s", [128, NTILE])
    slots_s = osb("slots_s", [128, NTILE])
    slotsi_s = osb("slotsi_s", [128, NTILE], I32)
    mu4_s = [osb(f"mu4_s{i}", [128, TPC]) for i in range(2)]
    rstd4_s = [osb(f"rstd4_s{i}", [128, TPC]) for i in range(2)]
    acc4_s = osb("acc4_s", [128, TPC])
    rstd24_s = osb("rstd24_s", [128, TPC])
    nmr4_s = osb("nmr4_s", [128, TPC])
    nmr24_s = osb("nmr24_s", [128, TPC])
    tt4_s = osb("tt4_s", [128, TPC])
    vv4_s = osb("vv4_s", [128, TPC])
    zsq_s = osb("zsq_s", [128, D], BF16)
    zps_s = osb("zps_s", [128, TPC, 2])
    z2s_s = osb("z2s_s", [128, TPC])
    be_s = osb("be_s", [128, 1])
    sm4_s = osb("sm4_s", [128, TPC])

    pk4 = [ops_(f"pk4_{i}", [128, CH]) for i in range(4)]
    po = [ops_(f"po{i}", [128, CH]) for i in range(2)]
    plg = ops_("plg", [32, CH])
    prt = ops_("prt", [128, 16, E])

    P.init_selfsems()
    sT = P.selfsem["tensor"]
    sLC = P.sem("sLC")
    sCast = P.sem("sCast")
    sXF = [P.sem(f"sXF{i}") for i in range(2)]
    sXT = [P.sem(f"sXT{i}") for i in range(2)]
    sWT = [P.sem(f"sWT{i}") for i in range(2)]
    sSP = [P.sem(f"sSP{i}") for i in range(2)]
    sSCT = [P.sem(f"sSCT{i}") for i in range(2)]
    sWE = [P.sem(f"sWE{i}") for i in range(2)]
    sBT = [P.sem(f"sBT{i}") for i in range(4)]
    sYW = [P.sem(f"sYW{i}") for i in range(3)]
    sYG = [P.sem(f"sYG{i}") for i in range(3)]
    sYL = [P.sem(f"sYL{i}") for i in range(3)]
    sOut = [P.sem(f"sOut{i}") for i in range(2)]

    p0ctx = contextlib.ExitStack()
    psb = lambda n, s, d=F32: p0ctx.enter_context(nc.sbuf_tensor(n, s, d))
    wof = psb("wof", [128, G, D], BF16)
    wrf = psb("wrf", [128, G, E], BF16)
    wrf32 = psb("wrf32", [128, G, E])
    csrow = psb("csrow", [1, E])

    for dst, src in ((tmk_s, tmk), (tmv_s, tmv), (tmr_s, tmr), (uu_s, uu),
                     (ww_s, ww), (iota_s, iotab), (ident_s, identf)):
        P.dma("sync", lambda q, d=dst, s=src: q.dma_start(out=d[:], in_=s), sLC)
    P.dma("sync", lambda q: q.dma_start(
        out=wrf32[:], in_=wrout.rearrange("(g p) e -> p g e", p=128)), sLC)
    P.dma("sync", lambda q: q.dma_start(
        out=rstm_s[:], in_=rstmk.rearrange("p (a e) -> p a e", e=E)), sLC)
    P.mark("p0_ld", sLC)

    P.dma("gpsimd", lambda q: q.dma_start(out=wkbf, in_=wkT), sCast)
    P.dma("gpsimd", lambda q: q.dma_start(out=wvbf, in_=wvT), sCast)
    P.dma("gpsimd", lambda q: q.dma_start(out=wrbf, in_=wrT), sCast)
    P.dma("gpsimd", lambda q: q.dma_start(out=tri_s[:], in_=trib), sCast)
    P.dma("gpsimd", lambda q: q.dma_start(
        out=wo_s[:], in_=woT.rearrange("(g p) d -> p g d", p=128)), sCast)
    P.dma("gpsimd", lambda q: q.dma_start(
        out=wr_s[:], in_=wrout.rearrange("(g p) e -> p g e", p=128)), sCast)
    P.dma("gpsimd", lambda q: q.dma_start(
        out=wof[:], in_=wo_n.rearrange("(g p) c -> p g c", p=128)), sCast)
    P.dma("gpsimd", lambda q: q.dma_start(
        out=wrf[:], in_=wrout.rearrange("(g p) e -> p g e", p=128)), sCast)
    P.mark("p0_casts", sCast)

    P.do("vector", lambda q: q.memset(onesrow_s[:], 1.0))
    P.do("vector", lambda q: q.memset(onecol_s[:], 1.0))
    P.do("vector", lambda q: q.memset(epscol_s[:], EPS))
    P.do("vector", lambda q: q.memset(runb_s[:], 0.0))
    P.do("vector", lambda q: q.memset(onescol_bf[:], 1.0), mark="p0_memset")

    P.wtm("scalar", "p0_ld")
    P.do("scalar", lambda q: q.activation(out=ewn_s[:], in_=ww_s[:], func=AF.Exp))
    P.do("scalar", lambda q: q.activation(out=ewn_s[:], in_=ewn_s[:], func=AF.Exp,
                                          scale=-1.0), mark="p0_scal")

    P.wtm("tensor", "p0_casts")
    P.wtm("tensor", "p0_ld")
    P.wtm("tensor", "p0_memset")
    for cm in range(G):
        for dg in range(G):
            P.do("tensor", lambda q, cm=cm, dg=dg: q.matmul(
                prt[:, 0, :], wof[:, dg, cm * 128:(cm + 1) * 128], wrf[:, dg, :],
                start=(dg == 0), stop=(dg == G - 1)),
                mark=(("p0mm", cm) if dg == G - 1 else None))
        P.wtm("vector", ("p0mm", cm))
        P.do("vector", lambda q, cm=cm: q.tensor_copy(m_s[:, cm, :], prt[:, 0, :]),
             mark=("p0ev", cm))
        P.wtm("tensor", ("p0ev", cm))
    for dg in range(G):
        P.do("tensor", lambda q, dg=dg: q.matmul(
            prt[0:1, 1, :], onecol_s[:], wrf32[:, dg, :],
            start=(dg == 0), stop=(dg == G - 1)),
            mark=("p0cs" if dg == G - 1 else None))
    P.wtm("vector", "p0cs")
    P.do("vector", lambda q: q.tensor_copy(csrow[:], prt[0:1, 1, :]), mark="p0csrow")
    P.wtm("tensor", "p0csrow")
    P.do("tensor", lambda q: q.matmul(prt[:, 2, :], onesrow_s[:], csrow[:],
                                      start=True, stop=True), mark="p0csb")
    P.wtm("vector", "p0csb")
    P.do("vector", lambda q: q.tensor_copy(cs_s[:], prt[:, 2, :]), mark="p0done_v")
    P.wtm("tensor", "p0done_v")
    p0ctx.close()

    actx = contextlib.ExitStack()
    asb = lambda n, s, d=F32: actx.enter_context(nc.sbuf_tensor(n, s, d))
    xfb = [asb(f"xfb{i}", [128, G, CH + 1], BF16) for i in range(2)]
    dx3 = asb("dx3", [128, G, CH], BF16)
    xkb = asb("xkb", [128, G, CH], BF16)
    xvb = asb("xvb", [128, G, CH], BF16)
    xrb = asb("xrb", [128, G, CH], BF16)
    Sbuf = asb("Sbuf", [128, G, CH + 1])
    Dbuf = asb("Dbuf", [128, G, CH + 1])
    NSLOT = 2
    ek_s = [asb(f"ek_s{i}", [128, CH]) for i in range(NSLOT)]
    er_s = [asb(f"er_s{i}", [128, CH]) for i in range(NSLOT)]
    ekv_s = [asb(f"ekv_s{i}", [128, CH]) for i in range(NSLOT)]
    num_s = [asb(f"num_s{i}", [128, CH]) for i in range(NSLOT)]
    den_s = [asb(f"den_s{i}", [128, CH]) for i in range(NSLOT)]
    rwkvb = [asb(f"rwkvb{i}", [128, G, CH], BF16) for i in range(2)]
    wkt_s = [asb(f"wkt_s{i}", [128, G, 128], BF16) for i in range(2)]
    wvt_s = [asb(f"wvt_s{i}", [128, G, 128], BF16) for i in range(2)]
    wrt_s = [asb(f"wrt_s{i}", [128, G, 128], BF16) for i in range(2)]
    xtok4 = asb("xtok4", [128, TPC, D])
    z4 = asb("z4", [128, TPC, D], BF16)
    x1b4 = [asb(f"x1b4_{i}", [128, TPC, D], BF16) for i in range(2)]
    sp_s = [asb(f"sp_s{i}", [128, D]) for i in range(2)]
    lgf_s = asb("lgf_s", [32, CH])
    lg4_s = asb("lg4_s", [128, TPC, E])
    mx4_s = asb("mx4_s", [128, TPC])
    nmx4_s = asb("nmx4_s", [128, TPC])
    ex_s = asb("ex_s", [128, E])
    ohr_s = asb("ohr_s", [128, TPC, E])
    pref_s = asb("pref_s", [128, TPC, E])
    oh_s = asb("oh_s", [128, TPC, E])
    ohb_s = asb("ohb_s", [128, TPC, E], BF16)
    scr3_s = asb("scr3_s", [128, TPC, E])

    def ldidx(c, m):
        return c * G + m

    def kvr_wtile_load(c, m):
        L = ldidx(c, m)
        wb = L % 2
        if L - 2 >= 0:
            P.wtm("scalar", ("proj", 2, (L - 2) // G, (L - 2) % G))
        for dst, src in ((wkt_s[wb], wkbf), (wvt_s[wb], wvbf), (wrt_s[wb], wrbf)):
            P.dma("scalar", lambda q, d=dst, s=src, m=m: q.dma_start(
                out=d[:], in_=s.rearrange("(g p) d -> p g d", p=128)[:, :, m * 128:(m + 1) * 128]),
                sWT[wb])
        P.mark(("wt", c, m), sWT[wb])

    def xfb_load(c):
        b, qh = divmod(c, NCH // COREB)
        t0 = qh * CH
        fb = xfb[c % 2]
        if c >= 2:
            P.wtm("gpsimd", ("mixready", c - 2))
            P.wtm("gpsimd", ("lgmm", c - 2))
        if qh == 0:
            P.dma("gpsimd", lambda q, b=b, fb=fb: q.dma_start(
                out=fb[:, :, 1:CH + 1],
                in_=xf[b].rearrange("(g p) t -> p g t", p=128)[:, :, 0:CH]),
                sXF[c % 2], mark=("xf", c))
        else:
            P.dma("gpsimd", lambda q, b=b, t0=t0, fb=fb: q.dma_start(
                out=fb[:, :, :],
                in_=xf[b].rearrange("(g p) t -> p g t", p=128)[:, :, t0 - 1:t0 + CH]),
                sXF[c % 2], mark=("xf", c))

    def p1_chunk(c):
        b, qh = divmod(c, NCH // COREB)
        fb = xfb[c % 2]
        rw = rwkvb[c % 2]

        if c + 1 < NCH:
            xfb_load(c + 1)

        if c >= 1:
            P.wtm("sync", ("zt", c - 1, TPC - 1))
        P.dma("sync", lambda q, c=c: q.dma_start(
            out=xtok4[:],
            in_=xt[c * CH:(c + 1) * CH, :].rearrange("(a p) d -> p a d", p=128)),
            sXT[c % 2], mark=("xt4", c))

        P.wtm("vector", ("xf", c))
        if qh == 0:
            P.do("vector", lambda q, fb=fb: q.memset(fb[:, :, 0:1], 0.0))
        if c >= 1:
            P.wtm("vector", ("proj", 2, c - 1, G - 1))
        if c >= 2:
            P.wtm("vector", ("lgmm", c - 2))
        P.do("vector", lambda q, fb=fb: q.tensor_tensor(
            out=dx3[:], in0=fb[:, :, 1:CH + 1], in1=fb[:, :, 0:CH], op=OP.subtract))
        for dst, tm in ((xkb, tmk_s), (xvb, tmv_s), (xrb, tmr_s)):
            P.do("vector", lambda q, dst=dst, tm=tm: q.tensor_tensor(
                out=dst[:], in0=dx3[:],
                in1=tm[:, :].unsqueeze(2).to_broadcast([128, G, CH]), op=OP.mult))
            P.do("vector", lambda q, dst=dst, fb=fb: q.tensor_tensor(
                out=dst[:], in0=dst[:], in1=fb[:, :, 0:CH], op=OP.add),
                mark=(("mixready", c) if dst is xrb else None))

        if c >= 1:
            P.wtm("vector", ("nd", c - 1, G - 1))
        if qh == 0:
            P.do("vector", lambda q: q.memset(Sbuf[:, :, 0:1], 0.0))
            P.do("vector", lambda q: q.memset(Dbuf[:, :, 0:1], 0.0), mark=("carry", c))
        else:
            P.do("vector", lambda q: q.tensor_copy(Sbuf[:, :, 0:1], Sbuf[:, :, CH:CH + 1]))
            P.do("vector", lambda q: q.tensor_copy(Dbuf[:, :, 0:1], Dbuf[:, :, CH:CH + 1]),
                 mark=("carry", c))

        P.wtm("tensor", ("mixready", c))

        def stage_T(m):
            si = ldidx(c, m)
            wb = si % 2
            if m + 1 < G:
                kvr_wtile_load(c, m + 1)
            elif c + 1 < NCH:
                kvr_wtile_load(c + 1, 0)
            P.wtm("tensor", ("wt", c, m))
            for pi, (wts, rhs) in enumerate(((wkt_s[wb], xkb), (wvt_s[wb], xvb),
                                             (wrt_s[wb], xrb))):
                u = si * 3 + pi
                bank = pk4[u % 4]
                if ("pjfree", u - 4) in marks:
                    P.wtm("tensor", ("pjfree", u - 4))
                for gg in range(G):
                    P.do("tensor", lambda q, w=wts, r=rhs, gg=gg, bank=bank: q.matmul(
                        bank[:], w[:, gg, :], r[:, gg, :],
                        start=(gg == 0), stop=(gg == G - 1)),
                        mark=(("proj", pi, c, m) if gg == G - 1 else None))

        def stage_S1(m):
            si = ldidx(c, m)
            slot = m % NSLOT
            kbank = pk4[(si * 3 + 0) % 4]
            rbank = pk4[(si * 3 + 2) % 4]
            prev = ("rwkv", c, m - NSLOT) if m - NSLOT >= 0 else \
                ("rwkv", c - 1, G + m - NSLOT)
            if prev in marks:
                P.wtm("scalar", prev)
            pn = ("nd", c, m - NSLOT) if m - NSLOT >= 0 else ("nd", c - 1, G + m - NSLOT)
            if pn in marks:
                P.wtm("scalar", pn)
            P.wtm("scalar", ("proj", 0, c, m))
            P.do("scalar", lambda q, slot=slot, kbank=kbank: q.activation(
                out=ek_s[slot][:], in_=kbank[:], func=AF.Exp))
            P.do("scalar", lambda q, slot=slot, m=m, kbank=kbank: q.activation(
                out=den_s[slot][:], in_=kbank[:], func=AF.Exp, bias=uu_s[:, m:m + 1]),
                mark=("euk", c, m))
            P.mark(("pjfree", si * 3 + 0), P.selfsem["scalar"])
            P.wtm("scalar", ("proj", 2, c, m))
            P.do("scalar", lambda q, slot=slot, rbank=rbank: q.activation(
                out=er_s[slot][:], in_=rbank[:], func=AF.Exp, scale=-1.0),
                mark=("er", c, m))
            P.mark(("pjfree", si * 3 + 2), P.selfsem["scalar"])

        def stage_V23(m):
            si = ldidx(c, m)
            slot = m % NSLOT
            vbank = pk4[(si * 3 + 1) % 4]
            pn = ("nd", c, m - NSLOT) if m - NSLOT >= 0 else ("nd", c - 1, G + m - NSLOT)
            if pn in marks:
                P.wtm("vector", pn)
            P.wtm("vector", ("proj", 1, c, m))
            P.wtm("vector", ("euk", c, m))
            P.do("vector", lambda q, slot=slot, vbank=vbank: q.tensor_tensor(
                out=ekv_s[slot][:], in0=ek_s[slot][:], in1=vbank[:], op=OP.mult))
            P.do("vector", lambda q, slot=slot, vbank=vbank: q.tensor_tensor(
                out=num_s[slot][:], in0=den_s[slot][:], in1=vbank[:], op=OP.mult),
                mark=("eukv", c, m))
            P.mark(("pjfree", si * 3 + 1), P.selfsem["vector"])
            P.wtm("vector", ("carry", c))
            P.do("vector", lambda q, slot=slot, m=m: q.tensor_tensor_scan(
                out=Sbuf[:, m, 1:CH + 1],
                data0=ewn_s[:, m:m + 1].to_broadcast([128, CH]),
                data1=ekv_s[slot][:], initial=Sbuf[:, m, 0:1],
                op0=OP.mult, op1=OP.add))
            P.do("vector", lambda q, slot=slot, m=m: q.tensor_tensor_scan(
                out=Dbuf[:, m, 1:CH + 1],
                data0=ewn_s[:, m:m + 1].to_broadcast([128, CH]),
                data1=ek_s[slot][:], initial=Dbuf[:, m, 0:1],
                op0=OP.mult, op1=OP.add), mark=("scan", c, m))

        def stage_G4(m):
            slot = m % NSLOT
            P.wtm("gpsimd", ("scan", c, m))
            P.wtm("gpsimd", ("eukv", c, m))
            P.wtm("gpsimd", ("er", c, m))
            prev = ("rwkv", c, m - NSLOT) if m - NSLOT >= 0 else \
                ("rwkv", c - 1, G + m - NSLOT)
            if prev in marks:
                P.wtm("gpsimd", prev)
            P.do("gpsimd", lambda q, slot=slot, m=m: q.tensor_tensor(
                out=num_s[slot][:], in0=num_s[slot][:], in1=Sbuf[:, m, 0:CH], op=OP.add))
            P.do("gpsimd", lambda q, slot=slot, m=m: q.tensor_tensor(
                out=den_s[slot][:], in0=den_s[slot][:], in1=Dbuf[:, m, 0:CH], op=OP.add),
                mark=("nd0", c, m))

        def stage_VF(m):
            slot = m % NSLOT
            P.wtm("vector", ("nd0", c, m))
            P.do("vector", lambda q, slot=slot: q.tensor_tensor(
                out=er_s[slot][:], in0=den_s[slot][:], in1=er_s[slot][:], op=OP.mult))
            P.do("vector", lambda q, slot=slot: q.tensor_tensor(
                out=den_s[slot][:], in0=den_s[slot][:], in1=er_s[slot][:], op=OP.add),
                mark=("nd", c, m))

        def stage_S5(m):
            slot = m % NSLOT
            P.wtm("scalar", ("nd", c, m))
            P.do("scalar", lambda q, slot=slot: q.activation(
                out=er_s[slot][:], in_=den_s[slot][:], func=AF.Ln))
            P.do("scalar", lambda q, slot=slot: q.activation(
                out=er_s[slot][:], in_=er_s[slot][:], func=AF.Exp, scale=-1.0),
                mark=("inv", c, m))

        def stage_V6(m):
            slot = m % NSLOT
            P.wtm("gpsimd", ("inv", c, m))
            P.do("gpsimd", lambda q, slot=slot, m=m, rw=rw: q.tensor_tensor(
                out=rw[:, m, :], in0=num_s[slot][:], in1=er_s[slot][:], op=OP.mult),
                mark=("rwkv", c, m))

        for m in range(G):
            if m >= 2:
                stage_V6(m - 2)
            stage_T(m)
            stage_S1(m)
            if m >= 1:
                stage_G4(m - 1)
            stage_V23(m)
            if m >= 1:
                stage_VF(m - 1)
                stage_S5(m - 1)
        stage_G4(G - 1)
        stage_VF(G - 1)
        stage_S5(G - 1)
        stage_V6(G - 2)
        stage_V6(G - 1)
        P.mark(("rwkvall", c), "gpsimd")

        P.wtm("tensor", ("rwkvall", c))
        for a in range(TPC):
            tile = c * TPC + a
            for h in range(2):
                oidx = a * 2 + h
                pb = po[oidx % 2]
                pk = ("ouse", c, oidx - 2) if oidx >= 2 else ("ouse", c - 1, 6 + oidx)
                if pk in marks:
                    P.wtm("tensor", pk)
                for g in range(G):
                    P.do("tensor", lambda q, g=g, a=a, h=h, pb=pb, rw=rw: q.matmul(
                        pb[:], rw[:, g, a * 128:(a + 1) * 128],
                        wo_s[:, g, h * CH:(h + 1) * CH],
                        start=(g == 0), stop=(g == G - 1)),
                        mark=(("omm", tile, h) if g == G - 1 else None))
                P.wtm("vector", ("omm", tile, h))
                if a == 0 and h == 0:
                    P.wtm("vector", ("xt4", c))
                    if c >= 1:
                        P.wtm("vector", ("x1w", c - 1))
                P.do("vector", lambda q, a=a, h=h, pb=pb: q.scalar_tensor_tensor(
                    out=z4[:, a, h * CH:(h + 1) * CH], in0=pb[:], scalar=1.0,
                    in1=xtok4[:, a, h * CH:(h + 1) * CH], op0=OP.mult, op1=OP.add,
                    accum_out=zps_s[:, a, h:h + 1]),
                    mark=("ouse", c, oidx))
            P.mark(("zt", c, a), P.selfsem["vector"])

        if c >= 1:
            P.wtm("tensor", ("lgev", c - 1))
        for g in range(G):
            P.do("tensor", lambda q, g=g, fb=fb: q.matmul(
                plg[:], wr_s[:, g, :], fb[:, g, 1:CH + 1],
                start=(g == 0), stop=False))
        for g in range(G):
            P.do("tensor", lambda q, g=g, rw=rw: q.matmul(
                plg[:], m_s[:, g, :], rw[:, g, :],
                start=False, stop=(g == G - 1)),
                mark=(("lgmm", c) if g == G - 1 else None))

        if c >= 1:
            emit_router_tensor1(c - 1)

        for a in range(TPC):
            P.wtm("scalar", ("zt", c, a))
            P.do("scalar", lambda q, a=a: q.activation(
                out=zsq_s[:], in_=z4[:, a, :], func=AF.Square,
                accum_out=z2s_s[:, a:a + 1]), mark=("z2", c, a))
        mu4 = mu4_s[c % 2]
        rstd4 = rstd4_s[c % 2]
        P.wtm("vector", ("z2", c, TPC - 1))
        if c >= 2:
            P.wtm("vector", ("gexp4", c - 2))
            P.wtm("vector", ("slots4", c - 2))
        P.do("vector", lambda q, mu4=mu4: q.tensor_tensor(
            out=mu4[:], in0=zps_s[:, :, 0], in1=zps_s[:, :, 1], op=OP.add))
        P.do("vector", lambda q, mu4=mu4: q.tensor_scalar(
            out=mu4[:], in0=mu4[:], scalar1=1.0 / D, scalar2=None, op0=OP.mult))
        P.do("vector", lambda q: q.tensor_scalar(
            out=tt4_s[:], in0=z2s_s[:], scalar1=1.0 / D, scalar2=None, op0=OP.mult))
        P.do("vector", lambda q, mu4=mu4: q.tensor_tensor(
            out=vv4_s[:], in0=mu4[:], in1=mu4[:], op=OP.mult))
        P.do("vector", lambda q: q.tensor_tensor(
            out=tt4_s[:], in0=tt4_s[:], in1=vv4_s[:], op=OP.subtract),
            mark=("var4", c))
        P.wtm("scalar", ("var4", c))
        if c >= 2:
            P.wtm("scalar", ("slots4", c - 2))
        P.do("scalar", lambda q, rstd4=rstd4: q.activation(
            out=rstd4[:], in_=tt4_s[:], func=AF.Ln, bias=epscol_s[:, 0:1]))
        P.do("scalar", lambda q, rstd4=rstd4: q.activation(
            out=rstd4[:], in_=rstd4[:], func=AF.Exp, scale=-0.5), mark=("rstd4", c))
        P.wtm("vector", ("rstd4", c))
        P.do("vector", lambda q, mu4=mu4, rstd4=rstd4: q.tensor_tensor(
            out=nmr4_s[:], in0=mu4[:], in1=rstd4[:], op=OP.mult))
        P.do("vector", lambda q: q.tensor_scalar(
            out=nmr4_s[:], in0=nmr4_s[:], scalar1=-1.0, scalar2=None, op0=OP.mult))
        P.do("vector", lambda q, rstd4=rstd4: q.tensor_scalar(
            out=rstd24_s[:], in0=rstd4[:], scalar1=2.0, scalar2=None, op0=OP.mult))
        P.do("vector", lambda q: q.tensor_scalar(
            out=nmr24_s[:], in0=nmr4_s[:], scalar1=2.0, scalar2=None, op0=OP.mult),
            mark=("lnc4", c))

        x1b = x1b4[c % 2]
        P.wtm("scalar", ("lnc4", c))
        if c >= 2:
            P.wtm("scalar", ("scat", c - 2, TPC - 1))
        for a in range(TPC):
            tile = c * TPC + a
            sps = sp_s[tile % 2]
            if tile >= 2:
                P.wtm("scalar", ("spw", tile - 2))
            P.do("scalar", lambda q, a=a, x1b=x1b, rstd4=rstd4: q.activation(
                out=x1b[:, a, :], in_=z4[:, a, :], func=AF.Identity,
                scale=rstd4[:, a:a + 1], bias=nmr4_s[:, a:a + 1]))
            P.do("scalar", lambda q, a=a, sps=sps: q.activation(
                out=sps[:], in_=z4[:, a, :], func=AF.Identity,
                scale=rstd24_s[:, a:a + 1], bias=nmr24_s[:, a:a + 1]),
                mark=("x1s", tile) if a < TPC - 1 else ("x1w", c))
            P.wtm("sync", ("x1s", tile) if a < TPC - 1 else ("x1w", c))
            P.dma("sync", lambda q, tile=tile, sps=sps: q.dma_start(
                out=x1sp[tile * 128:(tile + 1) * 128, :], in_=sps[:]), sSP[tile % 2],
                mark=("spw", tile))

        P.wtm("vector", ("lgmm", c))
        if c >= 1:
            P.wtm("vector", ("lgtr", c - 1))
        P.do("vector", lambda q: q.tensor_copy(lgf_s[:], plg[:]), mark=("lgev", c))

        if c >= 1:
            emit_router_rest(c - 1)

    def emit_router_tensor1(c):
        P.wtm("tensor", ("lgev", c))
        if c >= 1:
            P.wtm("tensor", ("slots4", c - 1))
        for a in range(TPC):
            P.do("tensor", lambda q, a=a: q.transpose(
                prt[:, a, :], lgf_s[:, a * 128:(a + 1) * 128], ident_s[0:32, 0:32]),
                mark=(("lgtr", c) if a == TPC - 1 else None))

    def emit_router_rest(c):
        P.wtm("vector", ("lgtr", c))
        P.wtm("vector", ("rtmm1", c - 1), required=False)
        P.wtm("vector", ("gexp4", c - 1), required=False)
        mu4 = mu4_s[c % 2]
        rstd4 = rstd4_s[c % 2]
        P.do("vector", lambda q, mu4=mu4: q.tensor_tensor(
            out=scr3_s[:], in0=mu4[:, :].unsqueeze(2).to_broadcast([128, TPC, E]),
            in1=cs_s[:, :].unsqueeze(1).to_broadcast([128, TPC, E]), op=OP.mult))
        P.do("vector", lambda q: q.tensor_tensor(
            out=lg4_s[:], in0=prt[:, 0:TPC, :], in1=scr3_s[:], op=OP.subtract))
        P.do("vector", lambda q, rstd4=rstd4: q.tensor_tensor(
            out=lg4_s[:], in0=lg4_s[:],
            in1=rstd4[:, :].unsqueeze(2).to_broadcast([128, TPC, E]), op=OP.mult),
            mark=("lgaff", c))
        P.do("vector", lambda q: q.tensor_reduce(
            out=mx4_s[:].unsqueeze(2), in_=lg4_s[:], axis=mybir.AxisListType.X,
            op=OP.max))
        P.do("vector", lambda q: q.tensor_scalar(
            out=nmx4_s[:], in0=mx4_s[:], scalar1=-1.0, scalar2=None, op0=OP.mult),
            mark=("nmx4", c))
        P.do("vector", lambda q: q.tensor_tensor(
            out=ohr_s[:], in0=lg4_s[:],
            in1=mx4_s[:, :].unsqueeze(2).to_broadcast([128, TPC, E]), op=OP.is_ge))
        P.do("vector", lambda q: q.tensor_tensor_scan(
            out=pref_s.rearrange("p a e -> p (a e)"), data0=rstm_s.rearrange("p a e -> p (a e)"),
            data1=ohr_s.rearrange("p a e -> p (a e)"), initial=0.0,
            op0=OP.mult, op1=OP.add))
        P.do("vector", lambda q: q.tensor_scalar(
            out=pref_s[:], in0=pref_s[:], scalar1=1.0, scalar2=None, op0=OP.is_equal))
        P.do("vector", lambda q: q.tensor_tensor(
            out=oh_s[:], in0=pref_s[:], in1=ohr_s[:], op=OP.mult))
        P.do("vector", lambda q: q.tensor_copy(ohb_s[:], oh_s[:]), mark=("oh4", c))
        for a in range(TPC):
            tile = c * TPC + a
            P.do("vector", lambda q, a=a, tile=tile: q.scalar_tensor_tensor(
                out=scr3_s[:, a, :], in0=oh_s[:, a, :], scalar=1.0, in1=iota_s[:],
                op0=OP.mult, op1=OP.mult, accum_out=eid_s[:, tile:tile + 1]),
                mark=(("eid4", c) if a == TPC - 1 else None))

        P.wtm("scalar", ("nmx4", c))
        P.wtm("scalar", ("slots4", c - 1), required=False)
        for a in range(TPC):
            P.do("scalar", lambda q, a=a: q.activation(
                out=ex_s[:], in_=lg4_s[:, a, :], func=AF.Exp, bias=nmx4_s[:, a:a + 1],
                accum_out=sm4_s[:, a:a + 1]),
                mark=(("gexp4", c) if a == TPC - 1 else None))

        P.wtm("tensor", ("oh4", c))
        for a in range(TPC):
            P.do("tensor", lambda q, a=a: q.matmul(
                prt[:, 4 + a, :], tri_s[:], ohb_s[:, a, :], start=True, stop=True))
            P.do("tensor", lambda q, a=a: q.matmul(
                prt[0:1, 12 + a, :], onescol_bf[:], ohb_s[:, a, :],
                start=True, stop=True),
                mark=(("rtmm1", c) if a == TPC - 1 else None))

        P.wtm("vector", ("rtmm1", c))
        for a in range(TPC):
            tile = c * TPC + a
            P.do("vector", lambda q, a=a, tile=tile: q.scalar_tensor_tensor(
                out=scr3_s[:, a, :], in0=oh_s[:, a, :], scalar=1.0, in1=prt[:, 4 + a, :],
                op0=OP.mult, op1=OP.mult, accum_out=lrank_s[:, tile:tile + 1]))
        P.do("vector", lambda q: q.tensor_copy(basesrow_s[0:1, 0:E], runb_s[:]))
        for a in range(1, TPC):
            P.do("vector", lambda q, a=a: q.tensor_tensor(
                out=basesrow_s[0:1, a * E:(a + 1) * E],
                in0=basesrow_s[0:1, (a - 1) * E:a * E],
                in1=prt[0:1, 12 + (a - 1), :], op=OP.add))
        P.do("vector", lambda q: q.tensor_tensor(
            out=runb_s[:], in0=basesrow_s[0:1, (TPC - 1) * E:TPC * E],
            in1=prt[0:1, 12 + TPC - 1, :], op=OP.add), mark=("bases", c))

        P.wtm("tensor", ("bases", c))
        P.do("tensor", lambda q: q.matmul(
            prt[:, 8:12, :].rearrange("p a e -> p (a e)"), onesrow_s[:],
            basesrow_s[:], start=True, stop=True), mark=("rtmm2", c))

        P.wtm("vector", ("rtmm2", c))
        P.wtm("vector", ("gexp4", c))
        for a in range(TPC):
            tile = c * TPC + a
            P.do("vector", lambda q, a=a, tile=tile: q.scalar_tensor_tensor(
                out=scr3_s[:, a, :], in0=oh_s[:, a, :], scalar=1.0, in1=prt[:, 8 + a, :],
                op0=OP.mult, op1=OP.mult, accum_out=be_s[:, 0:1]))
            P.do("vector", lambda q, a=a, tile=tile: q.tensor_tensor(
                out=lrank_s[:, tile:tile + 1], in0=lrank_s[:, tile:tile + 1],
                in1=be_s[:], op=OP.add))
        t0 = c * TPC
        sl4 = slots_s[:, t0:t0 + TPC]
        lr4 = lrank_s[:, t0:t0 + TPC]
        P.do("vector", lambda q, t0=t0, lr4=lr4: q.tensor_scalar(
            out=lr4, in0=lr4, scalar1=-1.0, scalar2=None, op0=OP.add))
        P.do("vector", lambda q, t0=t0: q.reciprocal(
            gate_s[:, t0:t0 + TPC], sm4_s[:]))
        P.do("vector", lambda q, t0=t0, sl4=sl4: q.tensor_scalar(
            out=sl4, in0=eid_s[:, t0:t0 + TPC], scalar1=float(CL), scalar2=None,
            op0=OP.mult))
        P.do("vector", lambda q, sl4=sl4, lr4=lr4: q.tensor_tensor(
            out=sl4, in0=sl4, in1=lr4, op=OP.add))
        P.do("vector", lambda q, lr4=lr4: q.tensor_scalar(
            out=acc4_s[:], in0=lr4, scalar1=float(CL), scalar2=None, op0=OP.is_lt))
        P.do("vector", lambda q, sl4=sl4: q.tensor_scalar(
            out=sl4, in0=sl4, scalar1=float(DUMP), scalar2=None, op0=OP.subtract))
        P.do("vector", lambda q, sl4=sl4: q.tensor_tensor(
            out=sl4, in0=sl4, in1=acc4_s[:], op=OP.mult))
        P.do("vector", lambda q, sl4=sl4: q.tensor_scalar(
            out=sl4, in0=sl4, scalar1=float(DUMP), scalar2=None, op0=OP.add))
        P.do("vector", lambda q, t0=t0, sl4=sl4: q.tensor_copy(
            slotsi_s[:, t0:t0 + TPC], sl4), mark=("slots4", c))

        P.wtm("gpsimd", ("slots4", c))
        P.wtm("gpsimd", ("x1w", c))
        x1b = x1b4[c % 2]
        for a in range(TPC):
            tile = c * TPC + a
            P.dma("gpsimd", lambda q, tile=tile, a=a, x1b=x1b: q.indirect_dma_start(
                out=bufq,
                out_offset=bass.IndirectOffsetOnAxis(ap=slotsi_s[:, tile:tile + 1], axis=0),
                in_=x1b[:, a, :], in_offset=None), sSCT[c % 2],
                mark=("scat", c, a))

    p0join = [P.selfsem["vector"], P.selfsem["scalar"], P.selfsem["tensor"],
              P.selfsem["gpsimd"], sLC, sCast]
    for _e in ENG:
        P.join(_e, p0join)
    xfb_load(0)
    kvr_wtile_load(0, 0)
    for c in range(NCH):
        p1_chunk(c)
    emit_router_tensor1(NCH - 1)
    emit_router_rest(NCH - 1)
    actx.close()

    bctx = contextlib.ExitStack()
    bsb = lambda n, s, d=F32: bctx.enter_context(nc.sbuf_tensor(n, s, d))
    wgt_s = [bsb(f"wgt_s{i}", [128, G, D], BF16) for i in range(2)]
    wut_s = [bsb(f"wut_s{i}", [128, G, D], BF16) for i in range(2)]
    wdt_s = [bsb(f"wdt_s{i}", [128, G, D], BF16) for i in range(2)]
    buf_s = [bsb(f"buf_s{i}", [128, G, CL], BF16) for i in range(2)]
    h_s = bsb("h_s", [128, G, CL], BF16)
    sg_s = bsb("sg_s", [128, CL], BF16)
    yst_s = [bsb(f"yst_s{i}", [128, D]) for i in range(3)]
    zrow_s = bsb("zrow_s", [1, D])
    pg4 = pk4[0][:, 0:CL]
    pu4 = pk4[1][:, 0:CL]
    py4 = [po[0], po[1]]

    alljoin = [P.selfsem[e] for e in ("vector", "scalar", "gpsimd", "tensor")] + \
        sXT + sXF + sSP + sSCT + sWT
    for e in ENG:
        P.join(e, alljoin)

    P.do("vector", lambda q: q.memset(zrow_s[:], 0.0), mark="zrow")
    P.wtm("gpsimd", "zrow")
    P.dma("gpsimd", lambda q: q.dma_start(out=ybuf[DUMP:DUMP + 1, :], in_=zrow_s[:]),
          sYW[0])

    def p4_weights(e):
        slot = e % 2
        if e >= 2:
            P.wtm("gpsimd", ("ymm", e - 2, STILE - 1, 1))
            P.wtm("gpsimd", ("umm", e - 2, G - 1))
        for dst, src in ((wgt_s[slot], wg), (wut_s[slot], wu), (wdt_s[slot], wd)):
            P.dma("gpsimd", lambda q, d=dst, s=src, e=e: q.dma_start(
                out=d[:], in_=s[e].rearrange("(g p) f -> p g f", p=128)), sWE[slot])
        P.mark(("we", e), sWE[slot])

    def p4_buf(e):
        slot = e % 2
        if e >= 2:
            P.wtm("sync", ("umm", e - 2, G - 1))
        for cg in range(G):
            P.dma("sync", lambda q, e=e, cg=cg, slot=slot: q.dma_start_transpose(
                buf_s[slot][:, cg, :],
                bufq[e * CL:(e + 1) * CL, cg * 128:(cg + 1) * 128]), sBT[slot * 2])
        P.mark(("bt_a", e), sBT[slot * 2])
        P.mark(("bt_b", e), sBT[slot * 2])

    def p4_expert(e):
        slot = e % 2
        P.wtm("tensor", ("we", e))
        P.wtm("tensor", ("bt_a", e))
        P.wtm("tensor", ("bt_b", e))
        for fg in range(G):
            if fg >= 1:
                P.wtm("tensor", ("silu", e, fg - 1))
                P.wtm("tensor", ("hmul", e, fg - 1))
            elif e >= 1:
                P.wtm("tensor", ("silu", e - 1, G - 1))
                P.wtm("tensor", ("hmul", e - 1, G - 1))
                P.wtm("tensor", ("ymm", e - 1, STILE - 1, 1))
            for cg in range(G):
                P.do("tensor", lambda q, fg=fg, cg=cg, slot=slot: q.matmul(
                    pg4, wgt_s[slot][:, cg, fg * 128:(fg + 1) * 128],
                    buf_s[slot][:, cg, :], start=(cg == 0), stop=(cg == G - 1)),
                    mark=(("gmm", e, fg) if cg == G - 1 else None))
            for cg in range(G):
                P.do("tensor", lambda q, fg=fg, cg=cg, slot=slot: q.matmul(
                    pu4, wut_s[slot][:, cg, fg * 128:(fg + 1) * 128],
                    buf_s[slot][:, cg, :], start=(cg == 0), stop=(cg == G - 1)),
                    mark=(("umm", e, fg) if cg == G - 1 else None))
            P.wtm("scalar", ("gmm", e, fg))
            if fg >= 1:
                P.wtm("scalar", ("hmul", e, fg - 1))
            elif e >= 1:
                P.wtm("scalar", ("hmul", e - 1, G - 1))
            P.do("scalar", lambda q: q.activation(out=sg_s[:], in_=pg4, func=AF.Silu),
                 mark=("silu", e, fg))
            P.wtm("vector", ("umm", e, fg))
            P.wtm("vector", ("silu", e, fg))
            if e >= 1 and fg == 0:
                P.wtm("vector", ("ymm", e - 1, STILE - 1, 1))
            P.do("vector", lambda q, fg=fg: q.tensor_tensor(
                out=h_s[:, fg, :], in0=sg_s[:], in1=pu4, op=OP.mult),
                mark=("hmul", e, fg))

        for st in range(STILE):
            kk = e * STILE + st
            for h in range(2):
                pb = py4[h]
                if ("yev", kk - 1) in marks:
                    P.wtm("tensor", ("yev", kk - 1))
                P.wtm("tensor", ("hmul", e, G - 1))
                for Fg in range(G):
                    P.do("tensor", lambda q, st=st, h=h, Fg=Fg, pb=pb, slot=slot: q.matmul(
                        pb[:], h_s[:, Fg, st * 128:(st + 1) * 128],
                        wdt_s[slot][:, Fg, h * CH:(h + 1) * CH],
                        start=(Fg == 0), stop=(Fg == G - 1)),
                        mark=(("ymm", e, st, h) if Fg == G - 1 else None))
            yslot = kk % 3
            if ("ystk", kk - 3) in marks:
                P.wtm("vector", ("ystk", kk - 3))
            for h in range(2):
                P.wtm("vector", ("ymm", e, st, h))
                P.do("vector", lambda q, h=h, yslot=yslot: q.tensor_copy(
                    yst_s[yslot][:, h * CH:(h + 1) * CH], py4[h][:]),
                    mark=(("yev", kk) if h == 1 else None))
            P.wtm("gpsimd", ("yev", kk))
            P.dma("gpsimd", lambda q, e=e, st=st, yslot=yslot: q.dma_start(
                out=ybuf[e * CL + st * 128:e * CL + (st + 1) * 128, :],
                in_=yst_s[yslot][:]), sYW[yslot], mark=("ystk", kk))

    p4_weights(0)
    p4_weights(1)
    for _s in sSCT:
        P.wt("sync", _s, _s.n)
    for e in range(E):
        p4_buf(e)
        p4_expert(e)
        if e + 2 < E:
            p4_weights(e + 2)

    alljoin2 = [P.selfsem[e] for e in ("vector", "scalar", "gpsimd", "tensor")] + \
        sWE + sBT + sYW
    for e in ENG:
        P.join(e, alljoin2)
    bctx.close()
    b5ctx = contextlib.ExitStack()
    bsb = lambda n, s, d=F32: b5ctx.enter_context(nc.sbuf_tensor(n, s, d))
    y_s = [bsb(f"y_s{i}", [128, TPC, D]) for i in range(2)]
    spld_s = [bsb(f"spld_s{i}", [128, TPC, D]) for i in range(2)]
    z24_s = bsb("z24_s", [128, TPC, D])
    o4_s = [bsb(f"o4_s{i}", [128, TPC, D]) for i in range(2)]
    z2q_s = bsb("z2q_s", [128, TPC])
    z2p_s = bsb("z2p_s", [128, TPC])
    p5mu_s = bsb("p5mu_s", [128, TPC])
    p5rs_s = bsb("p5rs_s", [128, TPC])
    p5nm_s = bsb("p5nm_s", [128, TPC])
    p5t_s = bsb("p5t_s", [128, TPC])
    p5v_s = bsb("p5v_s", [128, TPC])

    for _s in sYW:
        P.wt("gpsimd", _s, _s.n)
        P.wt("sync", _s, _s.n)

    def p5_chunk(c):
        cs = c % 2
        if c >= 2:
            P.wtm("gpsimd", ("p5z", c - 2))
            P.wtm("sync", ("p5z", c - 2))
        for a in range(TPC):
            tile = c * TPC + a
            P.dma("gpsimd", lambda q, tile=tile, a=a, cs=cs: q.indirect_dma_start(
                out=y_s[cs][:, a, :], out_offset=None, in_=ybuf,
                in_offset=bass.IndirectOffsetOnAxis(ap=slotsi_s[:, tile:tile + 1], axis=0)),
                sYG[cs], mark=("p5g", c))
        P.dma("sync", lambda q, c=c, cs=cs: q.dma_start(
            out=spld_s[cs][:],
            in_=x1sp[c * CH:(c + 1) * CH, :].rearrange("(a p) d -> p a d", p=128)),
            sYL[cs], mark=("p5l", c))
        if c >= 1:
            P.wtm("vector", ("p5o", c - 1))
        P.wtm("vector", ("p5g", c))
        P.wtm("vector", ("p5l", c))
        for a in range(TPC):
            tile = c * TPC + a
            P.do("vector", lambda q, tile=tile, a=a, cs=cs: q.scalar_tensor_tensor(
                out=z24_s[:, a, :], in0=y_s[cs][:, a, :],
                scalar=gate_s[:, tile:tile + 1], in1=spld_s[cs][:, a, :],
                op0=OP.mult, op1=OP.add, accum_out=z2p_s[:, a:a + 1]),
                mark=(("p5z", c) if a == TPC - 1 else None))
        for a in range(TPC):
            P.wtm("scalar", ("p5z", c))
            P.do("scalar", lambda q, a=a: q.activation(
                out=zsq_s[:], in_=z24_s[:, a, :], func=AF.Square,
                accum_out=z2q_s[:, a:a + 1]), mark=(("p5q", c) if a == TPC - 1 else None))
        P.wtm("vector", ("p5q", c))
        P.do("vector", lambda q: q.tensor_scalar(
            out=p5mu_s[:], in0=z2p_s[:], scalar1=1.0 / D, scalar2=None, op0=OP.mult))
        P.do("vector", lambda q: q.tensor_scalar(
            out=p5t_s[:], in0=z2q_s[:], scalar1=1.0 / D, scalar2=None, op0=OP.mult))
        P.do("vector", lambda q: q.tensor_tensor(
            out=p5v_s[:], in0=p5mu_s[:], in1=p5mu_s[:], op=OP.mult))
        P.do("vector", lambda q: q.tensor_tensor(
            out=p5t_s[:], in0=p5t_s[:], in1=p5v_s[:], op=OP.subtract),
            mark=("p5v", c))
        P.wtm("scalar", ("p5v", c))
        P.do("scalar", lambda q: q.activation(
            out=p5rs_s[:], in_=p5t_s[:], func=AF.Ln, bias=epscol_s[:, 0:1]))
        P.do("scalar", lambda q: q.activation(
            out=p5rs_s[:], in_=p5rs_s[:], func=AF.Exp, scale=-0.5), mark=("p5r", c))
        P.wtm("vector", ("p5r", c))
        P.do("vector", lambda q: q.tensor_tensor(
            out=p5nm_s[:], in0=p5mu_s[:], in1=p5rs_s[:], op=OP.mult))
        P.do("vector", lambda q: q.tensor_scalar(
            out=p5nm_s[:], in0=p5nm_s[:], scalar1=-1.0, scalar2=None, op0=OP.mult),
            mark=("p5c", c))
        oslot = c % 2
        if c >= 2:
            P.wtm("scalar", ("outw", c - 2))
        P.wtm("scalar", ("p5c", c))
        for a in range(TPC):
            P.do("scalar", lambda q, a=a, oslot=oslot: q.activation(
                out=o4_s[oslot][:, a, :], in_=z24_s[:, a, :], func=AF.Identity,
                scale=p5rs_s[:, a:a + 1], bias=p5nm_s[:, a:a + 1]),
                mark=(("p5o", c) if a == TPC - 1 else None))
        P.wtm("sync", ("p5o", c))
        P.dma("sync", lambda q, c=c, oslot=oslot: q.dma_start(
            out=out[c * CH:(c + 1) * CH, :].rearrange("(a p) d -> p a d", p=128),
            in_=o4_s[oslot][:]), sOut[oslot], mark=("outw", c))

    for c in range(NCH):
        p5_chunk(c)

    if debug:
        dbgr_s = bsb("dbgr_s", [128, NTILE, 4])
        for _s in sSP:
            P.wt("sync", _s, _s.n)
        P.dma("sync", lambda q: q.dma_start(out=dram["dbg_x1"], in_=x1sp), sOut[0])
        P.join("vector", [P.selfsem["vector"]])
        P.do("vector", lambda q: q.tensor_copy(dbgr_s[:, :, 0:1], eid_s[:].unsqueeze(2)))
        P.do("vector", lambda q: q.tensor_copy(dbgr_s[:, :, 1:2], lrank_s[:].unsqueeze(2)))
        P.do("vector", lambda q: q.tensor_copy(dbgr_s[:, :, 2:3], gate_s[:].unsqueeze(2)))
        P.do("vector", lambda q: q.tensor_copy(dbgr_s[:, :, 3:4], slots_s[:].unsqueeze(2)),
             mark="dbgr")
        P.wtm("sync", "dbgr")
        P.dma("sync", lambda q: q.dma_start(
            out=dram["dbg_route"].rearrange("(a p) f -> p a f", p=128),
            in_=dbgr_s[:]), sOut[0])
        P.dma("sync", lambda q: q.dma_start(out=dram["dbg_cnt"], in_=runb_s[:]), sOut[1])

    for _s in sOut:
        P.wt("sync", _s, _s.n)
    b5ctx.close()

    with octx:
        with nc.Block() as block:
            P.finalize(block)
    return nc


def _shard_inputs(inputs):
    x = np.asarray(inputs["L0_attn_x"], np.float32)
    Wk = np.asarray(inputs["L0_attn_Wk"], np.float32)
    Wv = np.asarray(inputs["L0_attn_Wv"], np.float32)
    Wr = np.asarray(inputs["L0_attn_Wr"], np.float32)
    Wo = np.ascontiguousarray(np.asarray(inputs["L0_attn_Wo"], np.float32))
    wrout = np.ascontiguousarray(np.asarray(inputs["L0_ffn_w_router"], np.float32))
    wgv = np.ascontiguousarray(np.asarray(inputs["L0_ffn_w_gate"], np.float32))
    wuv = np.ascontiguousarray(np.asarray(inputs["L0_ffn_w_up"], np.float32))
    wdv = np.ascontiguousarray(np.asarray(inputs["L0_ffn_w_down"], np.float32))
    pp = lambda v: np.ascontiguousarray(np.asarray(v, np.float32).reshape(G, 128).T)
    shared = {
        "wkT": np.ascontiguousarray(Wk.T), "wvT": np.ascontiguousarray(Wv.T),
        "wrT": np.ascontiguousarray(Wr.T), "woT": np.ascontiguousarray(Wo.T),
        "wo_n": Wo, "wrout": wrout, "wg": wgv, "wu": wuv, "wd": wdv,
        "tmk": pp(inputs["L0_attn_time_mix_k"]), "tmv": pp(inputs["L0_attn_time_mix_v"]),
        "tmr": pp(inputs["L0_attn_time_mix_r"]), "uu": pp(inputs["L0_attn_u"]),
        "ww": pp(inputs["L0_attn_w"]),
        "iotab": np.ascontiguousarray(
            np.broadcast_to(np.arange(E, dtype=np.float32), (128, E))),
        "identf": np.eye(128, dtype=np.float32),
        "trib": np.ascontiguousarray(np.triu(np.ones((128, 128), np.float32))),
        "rstmk": np.ascontiguousarray(np.broadcast_to(
            np.tile((np.arange(E) != 0).astype(np.float32), TPC), (128, TPC * E))),
    }
    in_maps = []
    for core in range(NCORE):
        xb = x[core * COREB:(core + 1) * COREB]
        m = dict(shared)
        m["xf"] = np.ascontiguousarray(xb.transpose(0, 2, 1))
        m["xt"] = np.ascontiguousarray(xb.reshape(TOK, D))
        in_maps.append(m)
    return in_maps


def kernel(**inputs):
    in_maps = _shard_inputs(inputs)
    nc = build_program(debug=False)
    res = run_bass_kernel_spmd(nc, in_maps, core_ids=list(range(NCORE)))
    outs = [np.asarray(r["out"], np.float32) for r in res.results]
    return np.concatenate(outs, axis=0).reshape(32, 2048, 1024)


# version 05 (final)
# speedup vs baseline: 1.0366x; 1.0366x over previous
import contextlib
import numpy as np
import concourse.bass as bass
import concourse.mybir as mybir
from concourse.bass_utils import run_bass_kernel_spmd
from concourse.alu_op_type import AluOpType as OP

F32 = mybir.dt.float32
BF16 = mybir.dt.bfloat16
I32 = mybir.dt.int32
AF = mybir.ActivationFunctionType

NCORE = 8
D = 1024
G = 8
E = 32
COREB = 4
T = 2048
TOK = COREB * T
CH = 512
NCH = TOK // CH
TPC = CH // 128
NTILE = TOK // 128
CL = 384
DUMP = E * CL
STILE = CL // 128
EPS = 1e-5

ENG = ("sync", "tensor", "vector", "scalar", "gpsimd")


class SemC:
    __slots__ = ("h", "n", "name")

    def __init__(self, h, name):
        self.h = h
        self.n = 0
        self.name = name


class Prog:
    CHAIN = ("vector", "scalar", "gpsimd")

    def __init__(self, nc, ctx):
        self.nc = nc
        self.ctx = ctx
        self.q = {e: [] for e in ENG}
        self.marks = {}
        self.selfsem = {}

    def init_selfsems(self):
        for e in self.CHAIN + ("tensor",):
            self.selfsem[e] = self.sem("self_" + e)

    def sem(self, name):
        return SemC(self.ctx.enter_context(self.nc.semaphore(name)), name)

    def do(self, eng, fn, mark=None, inc=None, n=1, dep=True):
        if eng in self.CHAIN:
            s = self.selfsem[eng]
            if dep:
                self.q[eng].append(lambda q: q.drain())
            if mark is not None:
                s.n += 1
                self.q[eng].append(lambda q, fn=fn, h=s.h: fn(q).then_inc(h, 1))
                self.marks[mark] = (s, s.n)
            else:
                self.q[eng].append(fn)
        elif eng == "tensor":
            s = self.selfsem[eng]
            if mark is not None:
                s.n += 1
                self.q[eng].append(lambda q, fn=fn, h=s.h: fn(q).then_inc(h, 1))
                self.marks[mark] = (s, s.n)
            else:
                self.q[eng].append(fn)
        else:
            raise ValueError(eng)

    def dma(self, eng, fn, sem, mark=None):
        sem.n += 16
        self.q[eng].append(lambda q, fn=fn, s=sem.h: fn(q).then_inc(s, 16))
        if mark is not None:
            self.marks[mark] = (sem, sem.n)

    def wt(self, eng, sem, val=None):
        v = sem.n if val is None else val
        if v > 0:
            self.q[eng].append(lambda q, s=sem.h, v=v: q.wait_ge(s, v))

    def mark(self, key, x):
        s = self.selfsem[x] if isinstance(x, str) else x
        self.marks[key] = (s, s.n)

    def wtm(self, eng, key, required=True):
        if key not in self.marks:
            assert not required, f"missing mark {key}"
            return
        s, v = self.marks[key]
        self.wt(eng, s, v)

    def join(self, eng, sems):
        for s in sems:
            self.wt(eng, s, s.n)

    def finalize(self, block):
        for e in ENG:
            ops = self.q[e]

            def body(q, ops=ops):
                for f in ops:
                    f(q)

            getattr(block, e)(body)


def build_program(debug=False):
    nc = bass.Bass("TRN2", target_bir_lowering=False)
    dram = {}

    def din(name, shape, dt=F32):
        dram[name] = nc.dram_tensor(name, shape, dt, kind="ExternalInput").ap()
        return dram[name]

    def dout(name, shape, dt=F32):
        dram[name] = nc.dram_tensor(name, shape, dt, kind="ExternalOutput").ap()
        return dram[name]

    def dint(name, shape, dt=F32):
        dram[name] = nc.dram_tensor(name, shape, dt, kind="Internal").ap()
        return dram[name]

    xf = din("xf", [COREB, D, T])
    xt = din("xt", [TOK, D])
    wkT = din("wkT", [D, D])
    wvT = din("wvT", [D, D])
    wrT = din("wrT", [D, D])
    woT = din("woT", [D, D])
    wo_n = din("wo_n", [D, D])
    wrout = din("wrout", [D, E])
    wg = din("wg", [E, D, D])
    wu = din("wu", [E, D, D])
    wd = din("wd", [E, D, D])
    tmk = din("tmk", [128, G])
    tmv = din("tmv", [128, G])
    tmr = din("tmr", [128, G])
    uu = din("uu", [128, G])
    ww = din("ww", [128, G])
    iotab = din("iotab", [128, E])
    identf = din("identf", [128, 128])
    trib = din("trib", [128, 128])
    rstmk = din("rstmk", [128, TPC * E])

    out = dout("out", [TOK, D])
    if debug:
        dout("dbg_x1", [TOK, D])
        dout("dbg_route", [NTILE * 128, 4])
        dout("dbg_cnt", [1, E])

    wkbf = dint("wkbf", [D, D], BF16)
    wvbf = dint("wvbf", [D, D], BF16)
    wrbf = dint("wrbf", [D, D], BF16)
    bufq = dint("bufq", [DUMP + 1, D], BF16)
    ybuf = dint("ybuf", [DUMP + 1, D])
    x1sp = dint("x1sp", [TOK, D])

    octx = contextlib.ExitStack()
    P = Prog(nc, octx)
    marks = P.marks
    osb = lambda n, s, d=F32: octx.enter_context(nc.sbuf_tensor(n, s, d))
    ops_ = lambda n, s, d=F32: octx.enter_context(nc.psum_tensor(n, s, d))

    tmk_s = osb("tmk_s", [128, G])
    tmv_s = osb("tmv_s", [128, G])
    tmr_s = osb("tmr_s", [128, G])
    uu_s = osb("uu_s", [128, G])
    ww_s = osb("ww_s", [128, G])
    ewn_s = osb("ewn_s", [128, G])
    eu_s = osb("eu_s", [128, G])
    iota_s = osb("iota_s", [128, E])
    ident_s = osb("ident_s", [128, 128])
    tri_s = osb("tri_s", [128, 128], BF16)
    rstm_s = osb("rstm_s", [128, TPC, E])
    onescol_bf = osb("onescol_bf", [128, 1], BF16)
    onesrow_s = osb("onesrow_s", [1, 128])
    onecol_s = osb("onecol_s", [128, 1])
    epscol_s = osb("epscol_s", [128, 1])
    cs_s = osb("cs_s", [128, E])
    wr_s = osb("wr_s", [128, G, E], BF16)
    m_s = osb("m_s", [128, G, E], BF16)
    wo_s = osb("wo_s", [128, G, D], BF16)
    runb_s = osb("runb_s", [1, E])
    basesrow_s = osb("basesrow_s", [1, TPC * E])
    eid_s = osb("eid_s", [128, NTILE])
    lrank_s = osb("lrank_s", [128, NTILE])
    gate_s = osb("gate_s", [128, NTILE])
    slots_s = osb("slots_s", [128, NTILE])
    slotsi_s = osb("slotsi_s", [128, NTILE], I32)
    mu4_s = [osb(f"mu4_s{i}", [128, TPC]) for i in range(2)]
    rstd4_s = [osb(f"rstd4_s{i}", [128, TPC]) for i in range(2)]
    acc4_s = osb("acc4_s", [128, TPC])
    rstd24_s = osb("rstd24_s", [128, TPC])
    nmr4_s = osb("nmr4_s", [128, TPC])
    nmr24_s = osb("nmr24_s", [128, TPC])
    tt4_s = osb("tt4_s", [128, TPC])
    vv4_s = osb("vv4_s", [128, TPC])
    zsq_s = osb("zsq_s", [128, D], BF16)
    zps_s = osb("zps_s", [128, TPC, 2])
    z2s_s = osb("z2s_s", [128, TPC])
    be_s = osb("be_s", [128, 1])
    sm4_s = osb("sm4_s", [128, TPC])

    pk4 = [ops_(f"pk4_{i}", [128, CH]) for i in range(4)]
    po = [ops_(f"po{i}", [128, CH]) for i in range(2)]
    plg = ops_("plg", [32, CH])
    prt = ops_("prt", [128, 16, E])

    P.init_selfsems()
    sT = P.selfsem["tensor"]
    sLC = P.sem("sLC")
    sCast = P.sem("sCast")
    sXF = [P.sem(f"sXF{i}") for i in range(2)]
    sXT = [P.sem(f"sXT{i}") for i in range(2)]
    sWT = [P.sem(f"sWT{i}") for i in range(2)]
    sSP = [P.sem(f"sSP{i}") for i in range(2)]
    sSCT = [P.sem(f"sSCT{i}") for i in range(2)]
    sWE = [P.sem(f"sWE{i}") for i in range(2)]
    sBT = [P.sem(f"sBT{i}") for i in range(4)]
    sYW = [P.sem(f"sYW{i}") for i in range(3)]
    sYG = [P.sem(f"sYG{i}") for i in range(3)]
    sYL = [P.sem(f"sYL{i}") for i in range(3)]
    sOut = [P.sem(f"sOut{i}") for i in range(2)]

    p0ctx = contextlib.ExitStack()
    psb = lambda n, s, d=F32: p0ctx.enter_context(nc.sbuf_tensor(n, s, d))
    wof = psb("wof", [128, G, D], BF16)
    wrf = psb("wrf", [128, G, E], BF16)
    wrf32 = psb("wrf32", [128, G, E])
    csrow = psb("csrow", [1, E])

    for dst, src in ((tmk_s, tmk), (tmv_s, tmv), (tmr_s, tmr), (uu_s, uu),
                     (ww_s, ww), (iota_s, iotab), (ident_s, identf)):
        P.dma("sync", lambda q, d=dst, s=src: q.dma_start(out=d[:], in_=s), sLC)
    P.dma("sync", lambda q: q.dma_start(
        out=wrf32[:], in_=wrout.rearrange("(g p) e -> p g e", p=128)), sLC)
    P.dma("sync", lambda q: q.dma_start(
        out=rstm_s[:], in_=rstmk.rearrange("p (a e) -> p a e", e=E)), sLC)
    P.mark("p0_ld", sLC)

    P.dma("gpsimd", lambda q: q.dma_start(out=wkbf, in_=wkT), sCast)
    P.dma("gpsimd", lambda q: q.dma_start(out=wvbf, in_=wvT), sCast)
    P.dma("gpsimd", lambda q: q.dma_start(out=wrbf, in_=wrT), sCast)
    P.dma("gpsimd", lambda q: q.dma_start(out=tri_s[:], in_=trib), sCast)
    P.dma("gpsimd", lambda q: q.dma_start(
        out=wo_s[:], in_=woT.rearrange("(g p) d -> p g d", p=128)), sCast)
    P.dma("gpsimd", lambda q: q.dma_start(
        out=wr_s[:], in_=wrout.rearrange("(g p) e -> p g e", p=128)), sCast)
    P.dma("gpsimd", lambda q: q.dma_start(
        out=wof[:], in_=wo_n.rearrange("(g p) c -> p g c", p=128)), sCast)
    P.dma("gpsimd", lambda q: q.dma_start(
        out=wrf[:], in_=wrout.rearrange("(g p) e -> p g e", p=128)), sCast)
    P.mark("p0_casts", sCast)

    P.do("vector", lambda q: q.memset(onesrow_s[:], 1.0))
    P.do("vector", lambda q: q.memset(onecol_s[:], 1.0))
    P.do("vector", lambda q: q.memset(epscol_s[:], EPS))
    P.do("vector", lambda q: q.memset(runb_s[:], 0.0))
    P.do("vector", lambda q: q.memset(onescol_bf[:], 1.0), mark="p0_memset")

    P.wtm("scalar", "p0_ld")
    P.do("scalar", lambda q: q.activation(out=ewn_s[:], in_=ww_s[:], func=AF.Exp))
    P.do("scalar", lambda q: q.activation(out=ewn_s[:], in_=ewn_s[:], func=AF.Exp,
                                          scale=-1.0))
    P.do("scalar", lambda q: q.activation(out=eu_s[:], in_=uu_s[:], func=AF.Exp),
         mark="p0_scal")

    P.wtm("tensor", "p0_casts")
    P.wtm("tensor", "p0_ld")
    P.wtm("tensor", "p0_memset")
    for cm in range(G):
        for dg in range(G):
            P.do("tensor", lambda q, cm=cm, dg=dg: q.matmul(
                prt[:, 0, :], wof[:, dg, cm * 128:(cm + 1) * 128], wrf[:, dg, :],
                start=(dg == 0), stop=(dg == G - 1)),
                mark=(("p0mm", cm) if dg == G - 1 else None))
        P.wtm("vector", ("p0mm", cm))
        P.do("vector", lambda q, cm=cm: q.tensor_copy(m_s[:, cm, :], prt[:, 0, :]),
             mark=("p0ev", cm))
        P.wtm("tensor", ("p0ev", cm))
    for dg in range(G):
        P.do("tensor", lambda q, dg=dg: q.matmul(
            prt[0:1, 1, :], onecol_s[:], wrf32[:, dg, :],
            start=(dg == 0), stop=(dg == G - 1)),
            mark=("p0cs" if dg == G - 1 else None))
    P.wtm("vector", "p0cs")
    P.do("vector", lambda q: q.tensor_copy(csrow[:], prt[0:1, 1, :]), mark="p0csrow")
    P.wtm("tensor", "p0csrow")
    P.do("tensor", lambda q: q.matmul(prt[:, 2, :], onesrow_s[:], csrow[:],
                                      start=True, stop=True), mark="p0csb")
    P.wtm("vector", "p0csb")
    P.do("vector", lambda q: q.tensor_copy(cs_s[:], prt[:, 2, :]), mark="p0done_v")
    P.wtm("tensor", "p0done_v")
    p0ctx.close()

    actx = contextlib.ExitStack()
    asb = lambda n, s, d=F32: actx.enter_context(nc.sbuf_tensor(n, s, d))
    xfb = [asb(f"xfb{i}", [128, G, CH + 1], BF16) for i in range(2)]
    dx3 = asb("dx3", [128, G, CH], BF16)
    xkb = asb("xkb", [128, G, CH], BF16)
    xvb = asb("xvb", [128, G, CH], BF16)
    xrb = asb("xrb", [128, G, CH], BF16)
    Sbuf = asb("Sbuf", [128, G, CH + 1])
    Dbuf = asb("Dbuf", [128, G, CH + 1])
    NSLOT = 2
    ek_s = [asb(f"ek_s{i}", [128, CH]) for i in range(NSLOT)]
    er_s = [asb(f"er_s{i}", [128, CH]) for i in range(NSLOT)]
    ekv_s = [asb(f"ekv_s{i}", [128, CH]) for i in range(NSLOT)]
    num_s = [asb(f"num_s{i}", [128, CH]) for i in range(NSLOT)]
    den_s = [asb(f"den_s{i}", [128, CH]) for i in range(NSLOT)]
    rwkvb = [asb(f"rwkvb{i}", [128, G, CH], BF16) for i in range(2)]
    wkt_s = [asb(f"wkt_s{i}", [128, G, 128], BF16) for i in range(2)]
    wvt_s = [asb(f"wvt_s{i}", [128, G, 128], BF16) for i in range(2)]
    wrt_s = [asb(f"wrt_s{i}", [128, G, 128], BF16) for i in range(2)]
    xtok4 = asb("xtok4", [128, TPC, D])
    z4 = asb("z4", [128, TPC, D], BF16)
    x1b4 = [asb(f"x1b4_{i}", [128, TPC, D], BF16) for i in range(2)]
    sp_s = [asb(f"sp_s{i}", [128, D]) for i in range(2)]
    lgf_s = asb("lgf_s", [32, CH])
    lg4_s = asb("lg4_s", [128, TPC, E])
    mx4_s = asb("mx4_s", [128, TPC])
    nmx4_s = asb("nmx4_s", [128, TPC])
    ex_s = asb("ex_s", [128, E])
    ohr_s = asb("ohr_s", [128, TPC, E])
    pref_s = asb("pref_s", [128, TPC, E])
    oh_s = asb("oh_s", [128, TPC, E])
    ohb_s = asb("ohb_s", [128, TPC, E], BF16)
    scr3_s = asb("scr3_s", [128, TPC, E])

    def ldidx(c, m):
        return c * G + m

    def kvr_wtile_load(c, m):
        L = ldidx(c, m)
        wb = L % 2
        if L - 2 >= 0:
            P.wtm("scalar", ("proj", 2, (L - 2) // G, (L - 2) % G))
        for dst, src in ((wkt_s[wb], wkbf), (wvt_s[wb], wvbf), (wrt_s[wb], wrbf)):
            P.dma("scalar", lambda q, d=dst, s=src, m=m: q.dma_start(
                out=d[:], in_=s.rearrange("(g p) d -> p g d", p=128)[:, :, m * 128:(m + 1) * 128]),
                sWT[wb])
        P.mark(("wt", c, m), sWT[wb])

    def xfb_load(c):
        b, qh = divmod(c, NCH // COREB)
        t0 = qh * CH
        fb = xfb[c % 2]
        if c >= 2:
            P.wtm("gpsimd", ("mixready", c - 2))
            P.wtm("gpsimd", ("lgmm", c - 2))
        if qh == 0:
            P.dma("gpsimd", lambda q, b=b, fb=fb: q.dma_start(
                out=fb[:, :, 1:CH + 1],
                in_=xf[b].rearrange("(g p) t -> p g t", p=128)[:, :, 0:CH]),
                sXF[c % 2], mark=("xf", c))
        else:
            P.dma("gpsimd", lambda q, b=b, t0=t0, fb=fb: q.dma_start(
                out=fb[:, :, :],
                in_=xf[b].rearrange("(g p) t -> p g t", p=128)[:, :, t0 - 1:t0 + CH]),
                sXF[c % 2], mark=("xf", c))

    def emit_A(c):
        b, qh = divmod(c, NCH // COREB)
        fb = xfb[c % 2]
        rw = rwkvb[c % 2]

        P.wtm("vector", ("xf", c))
        if qh == 0:
            P.do("vector", lambda q, fb=fb: q.memset(fb[:, :, 0:1], 0.0))
        if c >= 1:
            P.wtm("vector", ("proj", 2, c - 1, G - 1))
        if c >= 2:
            P.wtm("vector", ("lgmm", c - 2))
        P.do("vector", lambda q, fb=fb: q.tensor_tensor(
            out=dx3[:], in0=fb[:, :, 1:CH + 1], in1=fb[:, :, 0:CH], op=OP.subtract))
        for dst, tm in ((xkb, tmk_s), (xvb, tmv_s), (xrb, tmr_s)):
            P.do("vector", lambda q, dst=dst, tm=tm: q.tensor_tensor(
                out=dst[:], in0=dx3[:],
                in1=tm[:, :].unsqueeze(2).to_broadcast([128, G, CH]), op=OP.mult))
            P.do("vector", lambda q, dst=dst, fb=fb: q.tensor_tensor(
                out=dst[:], in0=dst[:], in1=fb[:, :, 0:CH], op=OP.add),
                mark=(("mixready", c) if dst is xrb else None))

        if c >= 1:
            P.wtm("vector", ("nd", c - 1, G - 1))
        if qh == 0:
            P.do("vector", lambda q: q.memset(Sbuf[:, :, 0:1], 0.0))
            P.do("vector", lambda q: q.memset(Dbuf[:, :, 0:1], 0.0), mark=("carry", c))
        else:
            P.do("vector", lambda q: q.tensor_copy(Sbuf[:, :, 0:1], Sbuf[:, :, CH:CH + 1]))
            P.do("vector", lambda q: q.tensor_copy(Dbuf[:, :, 0:1], Dbuf[:, :, CH:CH + 1]),
                 mark=("carry", c))

        P.wtm("tensor", ("mixready", c))

        def stage_T(m):
            si = ldidx(c, m)
            wb = si % 2
            if m + 1 < G:
                kvr_wtile_load(c, m + 1)
            elif c + 1 < NCH:
                kvr_wtile_load(c + 1, 0)
            P.wtm("tensor", ("wt", c, m))
            for pi, (wts, rhs) in enumerate(((wkt_s[wb], xkb), (wvt_s[wb], xvb),
                                             (wrt_s[wb], xrb))):
                u = si * 3 + pi
                bank = pk4[u % 4]
                if ("pjfree", u - 4) in marks:
                    P.wtm("tensor", ("pjfree", u - 4))
                for gg in range(G):
                    P.do("tensor", lambda q, w=wts, r=rhs, gg=gg, bank=bank: q.matmul(
                        bank[:], w[:, gg, :], r[:, gg, :],
                        start=(gg == 0), stop=(gg == G - 1)),
                        mark=(("proj", pi, c, m) if gg == G - 1 else None))

        def stage_S1(m):
            si = ldidx(c, m)
            slot = m % NSLOT
            kbank = pk4[(si * 3 + 0) % 4]
            rbank = pk4[(si * 3 + 2) % 4]
            prev = ("rwkv", c, m - NSLOT) if m - NSLOT >= 0 else \
                ("rwkv", c - 1, G + m - NSLOT)
            if prev in marks:
                P.wtm("scalar", prev)
            pn = ("nd", c, m - NSLOT) if m - NSLOT >= 0 else ("nd", c - 1, G + m - NSLOT)
            if pn in marks:
                P.wtm("scalar", pn)
            P.wtm("scalar", ("proj", 0, c, m))
            P.do("scalar", lambda q, slot=slot, kbank=kbank: q.activation(
                out=ek_s[slot][:], in_=kbank[:], func=AF.Exp), mark=("ek", c, m),
                dep=False)
            P.mark(("pjfree", si * 3 + 0), P.selfsem["scalar"])
            P.wtm("scalar", ("proj", 2, c, m))
            P.do("scalar", lambda q, slot=slot, rbank=rbank: q.activation(
                out=er_s[slot][:], in_=rbank[:], func=AF.Exp, scale=-1.0),
                mark=("er", c, m), dep=False)
            P.mark(("pjfree", si * 3 + 2), P.selfsem["scalar"])

        def stage_V23(m):
            si = ldidx(c, m)
            slot = m % NSLOT
            vbank = pk4[(si * 3 + 1) % 4]
            pn = ("nd", c, m - NSLOT) if m - NSLOT >= 0 else ("nd", c - 1, G + m - NSLOT)
            if pn in marks:
                P.wtm("vector", pn)
            P.wtm("vector", ("proj", 1, c, m))
            P.wtm("vector", ("ek", c, m))
            P.do("vector", lambda q, slot=slot, vbank=vbank: q.tensor_tensor(
                out=ekv_s[slot][:], in0=ek_s[slot][:], in1=vbank[:], op=OP.mult),
                mark=("ekv", c, m), dep=False)
            P.do("vector", lambda q, slot=slot, m=m: q.tensor_scalar(
                out=num_s[slot][:], in0=ekv_s[slot][:], scalar1=eu_s[:, m:m + 1],
                scalar2=None, op0=OP.mult))
            P.do("vector", lambda q, slot=slot, m=m: q.tensor_scalar(
                out=den_s[slot][:], in0=ek_s[slot][:], scalar1=eu_s[:, m:m + 1],
                scalar2=None, op0=OP.mult), dep=False)
            marks[("pjfree", si * 3 + 1)] = marks[("ekv", c, m)]
            P.wtm("vector", ("carry", c))
            P.do("vector", lambda q, slot=slot, m=m: q.tensor_tensor_scan(
                out=Sbuf[:, m, 1:CH + 1],
                data0=ewn_s[:, m:m + 1].to_broadcast([128, CH]),
                data1=ekv_s[slot][:], initial=Sbuf[:, m, 0:1],
                op0=OP.mult, op1=OP.add), dep=False)
            P.do("vector", lambda q, slot=slot, m=m: q.tensor_tensor_scan(
                out=Dbuf[:, m, 1:CH + 1],
                data0=ewn_s[:, m:m + 1].to_broadcast([128, CH]),
                data1=ek_s[slot][:], initial=Dbuf[:, m, 0:1],
                op0=OP.mult, op1=OP.add), mark=("scan", c, m), dep=False)

        def stage_G4(m):
            slot = m % NSLOT
            P.wtm("gpsimd", ("scan", c, m))
            prev = ("rwkv", c, m - NSLOT) if m - NSLOT >= 0 else \
                ("rwkv", c - 1, G + m - NSLOT)
            if prev in marks:
                P.wtm("gpsimd", prev)
            P.do("gpsimd", lambda q, slot=slot, m=m: q.tensor_tensor(
                out=num_s[slot][:], in0=num_s[slot][:], in1=Sbuf[:, m, 0:CH], op=OP.add))
            P.do("gpsimd", lambda q, slot=slot, m=m: q.tensor_tensor(
                out=den_s[slot][:], in0=den_s[slot][:], in1=Dbuf[:, m, 0:CH], op=OP.add),
                mark=("nd0", c, m), dep=False)

        def stage_VF(m):
            slot = m % NSLOT
            P.wtm("vector", ("er", c, m))
            P.do("vector", lambda q, slot=slot: q.tensor_scalar(
                out=er_s[slot][:], in0=er_s[slot][:], scalar1=1.0, scalar2=None,
                op0=OP.add), dep=False)
            P.wtm("vector", ("nd0", c, m))
            P.do("vector", lambda q, slot=slot: q.tensor_tensor(
                out=den_s[slot][:], in0=den_s[slot][:], in1=er_s[slot][:], op=OP.mult),
                mark=("nd", c, m))

        def stage_S5(m):
            slot = m % NSLOT
            P.wtm("scalar", ("nd", c, m))
            P.do("scalar", lambda q, slot=slot: q.activation(
                out=er_s[slot][:], in_=den_s[slot][:], func=AF.Ln), dep=False)
            P.do("scalar", lambda q, slot=slot: q.activation(
                out=er_s[slot][:], in_=er_s[slot][:], func=AF.Exp, scale=-1.0),
                mark=("inv", c, m))

        def stage_V6(m):
            slot = m % NSLOT
            P.wtm("gpsimd", ("inv", c, m))
            P.do("gpsimd", lambda q, slot=slot, m=m, rw=rw: q.tensor_tensor(
                out=rw[:, m, :], in0=num_s[slot][:], in1=er_s[slot][:], op=OP.mult),
                mark=("rwkv", c, m))

        for m in range(G):
            if m >= 2:
                stage_V6(m - 2)
            stage_T(m)
            stage_S1(m)
            if m >= 1:
                stage_G4(m - 1)
            stage_V23(m)
            if m >= 1:
                stage_VF(m - 1)
                stage_S5(m - 1)
        stage_G4(G - 1)
        stage_VF(G - 1)
        stage_S5(G - 1)
        stage_V6(G - 2)
        stage_V6(G - 1)
        P.mark(("rwkvall", c), "gpsimd")

    def emit_B(c):
        b, qh = divmod(c, NCH // COREB)
        fb = xfb[c % 2]
        rw = rwkvb[c % 2]
        if c >= 1:
            P.wtm("sync", ("zt", c - 1, TPC - 1))
        P.dma("sync", lambda q, c=c: q.dma_start(
            out=xtok4[:],
            in_=xt[c * CH:(c + 1) * CH, :].rearrange("(a p) d -> p a d", p=128)),
            sXT[c % 2], mark=("xt4", c))

        P.wtm("tensor", ("rwkvall", c))
        for a in range(TPC):
            tile = c * TPC + a
            for h in range(2):
                oidx = a * 2 + h
                pb = po[oidx % 2]
                pk = ("ouse", c, oidx - 2) if oidx >= 2 else ("ouse", c - 1, 6 + oidx)
                if pk in marks:
                    P.wtm("tensor", pk)
                for g in range(G):
                    P.do("tensor", lambda q, g=g, a=a, h=h, pb=pb, rw=rw: q.matmul(
                        pb[:], rw[:, g, a * 128:(a + 1) * 128],
                        wo_s[:, g, h * CH:(h + 1) * CH],
                        start=(g == 0), stop=(g == G - 1)),
                        mark=(("omm", tile, h) if g == G - 1 else None))
                P.wtm("vector", ("omm", tile, h))
                if a == 0 and h == 0:
                    P.wtm("vector", ("xt4", c))
                    if c >= 1:
                        P.wtm("vector", ("x1w", c - 1))
                P.do("vector", lambda q, a=a, h=h, pb=pb: q.scalar_tensor_tensor(
                    out=z4[:, a, h * CH:(h + 1) * CH], in0=pb[:], scalar=1.0,
                    in1=xtok4[:, a, h * CH:(h + 1) * CH], op0=OP.mult, op1=OP.add,
                    accum_out=zps_s[:, a, h:h + 1]),
                    mark=("ouse", c, oidx), dep=False)
            P.mark(("zt", c, a), P.selfsem["vector"])

        if c >= 1:
            P.wtm("tensor", ("lgev", c - 1))
        for g in range(G):
            P.do("tensor", lambda q, g=g, fb=fb: q.matmul(
                plg[:], wr_s[:, g, :], fb[:, g, 1:CH + 1],
                start=(g == 0), stop=False))
        for g in range(G):
            P.do("tensor", lambda q, g=g, rw=rw: q.matmul(
                plg[:], m_s[:, g, :], rw[:, g, :],
                start=False, stop=(g == G - 1)),
                mark=(("lgmm", c) if g == G - 1 else None))

        for a in range(TPC):
            P.wtm("scalar", ("zt", c, a))
            P.do("scalar", lambda q, a=a: q.activation(
                out=zsq_s[:], in_=z4[:, a, :], func=AF.Square,
                accum_out=z2s_s[:, a:a + 1]), mark=("z2", c, a))
        mu4 = mu4_s[c % 2]
        rstd4 = rstd4_s[c % 2]
        P.wtm("vector", ("z2", c, TPC - 1))
        if c >= 2:
            P.wtm("vector", ("gexp4", c - 2))
            P.wtm("vector", ("slots4", c - 2))
        P.do("vector", lambda q, mu4=mu4: q.tensor_tensor(
            out=mu4[:], in0=zps_s[:, :, 0], in1=zps_s[:, :, 1], op=OP.add))
        P.do("vector", lambda q, mu4=mu4: q.tensor_scalar(
            out=mu4[:], in0=mu4[:], scalar1=1.0 / D, scalar2=None, op0=OP.mult))
        P.do("vector", lambda q: q.tensor_scalar(
            out=tt4_s[:], in0=z2s_s[:], scalar1=1.0 / D, scalar2=None, op0=OP.mult))
        P.do("vector", lambda q, mu4=mu4: q.tensor_tensor(
            out=vv4_s[:], in0=mu4[:], in1=mu4[:], op=OP.mult))
        P.do("vector", lambda q: q.tensor_tensor(
            out=tt4_s[:], in0=tt4_s[:], in1=vv4_s[:], op=OP.subtract),
            mark=("var4", c))
        P.wtm("scalar", ("var4", c))
        if c >= 2:
            P.wtm("scalar", ("slots4", c - 2))
        P.do("scalar", lambda q, rstd4=rstd4: q.activation(
            out=rstd4[:], in_=tt4_s[:], func=AF.Ln, bias=epscol_s[:, 0:1]))
        P.do("scalar", lambda q, rstd4=rstd4: q.activation(
            out=rstd4[:], in_=rstd4[:], func=AF.Exp, scale=-0.5), mark=("rstd4", c))
        P.wtm("vector", ("rstd4", c))
        P.do("vector", lambda q, mu4=mu4, rstd4=rstd4: q.tensor_tensor(
            out=nmr4_s[:], in0=mu4[:], in1=rstd4[:], op=OP.mult))
        P.do("vector", lambda q: q.tensor_scalar(
            out=nmr4_s[:], in0=nmr4_s[:], scalar1=-1.0, scalar2=None, op0=OP.mult))
        P.do("vector", lambda q, rstd4=rstd4: q.tensor_scalar(
            out=rstd24_s[:], in0=rstd4[:], scalar1=2.0, scalar2=None, op0=OP.mult))
        P.do("vector", lambda q: q.tensor_scalar(
            out=nmr24_s[:], in0=nmr4_s[:], scalar1=2.0, scalar2=None, op0=OP.mult),
            mark=("lnc4", c))

        x1b = x1b4[c % 2]
        P.wtm("scalar", ("lnc4", c))
        for back in (2, 3):
            if ("scat", c - back, TPC - 1) in marks:
                P.wtm("scalar", ("scat", c - back, TPC - 1))
                break
        for a in range(TPC):
            tile = c * TPC + a
            sps = sp_s[tile % 2]
            if tile >= 2:
                P.wtm("scalar", ("spw", tile - 2))
            P.do("scalar", lambda q, a=a, x1b=x1b, rstd4=rstd4: q.activation(
                out=x1b[:, a, :], in_=z4[:, a, :], func=AF.Identity,
                scale=rstd4[:, a:a + 1], bias=nmr4_s[:, a:a + 1]))
            P.do("scalar", lambda q, a=a, sps=sps: q.activation(
                out=sps[:], in_=z4[:, a, :], func=AF.Identity,
                scale=rstd24_s[:, a:a + 1], bias=nmr24_s[:, a:a + 1]),
                mark=("x1s", tile) if a < TPC - 1 else ("x1w", c))
            P.wtm("sync", ("x1s", tile) if a < TPC - 1 else ("x1w", c))
            P.dma("sync", lambda q, tile=tile, sps=sps: q.dma_start(
                out=x1sp[tile * 128:(tile + 1) * 128, :], in_=sps[:]), sSP[tile % 2],
                mark=("spw", tile))

        P.wtm("vector", ("lgmm", c))
        if c >= 1:
            P.wtm("vector", ("lgtr", c - 1))
        P.do("vector", lambda q: q.tensor_copy(lgf_s[:], plg[:]), mark=("lgev", c))


    def emit_router_tensor1(c):
        P.wtm("tensor", ("lgev", c))
        if c >= 1:
            P.wtm("tensor", ("slots4", c - 1))
        for a in range(TPC):
            P.do("tensor", lambda q, a=a: q.transpose(
                prt[:, a, :], lgf_s[:, a * 128:(a + 1) * 128], ident_s[0:32, 0:32]),
                mark=(("lgtr", c) if a == TPC - 1 else None))

    def emit_router_rest(c):
        P.wtm("vector", ("lgtr", c))
        P.wtm("vector", ("rtmm1", c - 1), required=False)
        P.wtm("vector", ("gexp4", c - 1), required=False)
        mu4 = mu4_s[c % 2]
        rstd4 = rstd4_s[c % 2]
        P.do("vector", lambda q, mu4=mu4: q.tensor_tensor(
            out=scr3_s[:], in0=mu4[:, :].unsqueeze(2).to_broadcast([128, TPC, E]),
            in1=cs_s[:, :].unsqueeze(1).to_broadcast([128, TPC, E]), op=OP.mult))
        P.do("vector", lambda q: q.tensor_tensor(
            out=lg4_s[:], in0=prt[:, 0:TPC, :], in1=scr3_s[:], op=OP.subtract))
        P.do("vector", lambda q, rstd4=rstd4: q.tensor_tensor(
            out=lg4_s[:], in0=lg4_s[:],
            in1=rstd4[:, :].unsqueeze(2).to_broadcast([128, TPC, E]), op=OP.mult),
            mark=("lgaff", c))
        P.do("vector", lambda q: q.tensor_reduce(
            out=mx4_s[:].unsqueeze(2), in_=lg4_s[:], axis=mybir.AxisListType.X,
            op=OP.max))
        P.do("vector", lambda q: q.tensor_scalar(
            out=nmx4_s[:], in0=mx4_s[:], scalar1=-1.0, scalar2=None, op0=OP.mult),
            mark=("nmx4", c))
        P.do("vector", lambda q: q.tensor_tensor(
            out=ohr_s[:], in0=lg4_s[:],
            in1=mx4_s[:, :].unsqueeze(2).to_broadcast([128, TPC, E]), op=OP.is_ge))
        P.do("vector", lambda q: q.tensor_tensor_scan(
            out=pref_s.rearrange("p a e -> p (a e)"), data0=rstm_s.rearrange("p a e -> p (a e)"),
            data1=ohr_s.rearrange("p a e -> p (a e)"), initial=0.0,
            op0=OP.mult, op1=OP.add))
        P.do("vector", lambda q: q.tensor_scalar(
            out=pref_s[:], in0=pref_s[:], scalar1=1.0, scalar2=None, op0=OP.is_equal))
        P.do("vector", lambda q: q.tensor_tensor(
            out=oh_s[:], in0=pref_s[:], in1=ohr_s[:], op=OP.mult))
        P.do("vector", lambda q: q.tensor_copy(ohb_s[:], oh_s[:]), mark=("oh4", c))
        for a in range(TPC):
            tile = c * TPC + a
            P.do("vector", lambda q, a=a, tile=tile: q.scalar_tensor_tensor(
                out=scr3_s[:, a, :], in0=oh_s[:, a, :], scalar=1.0, in1=iota_s[:],
                op0=OP.mult, op1=OP.mult, accum_out=eid_s[:, tile:tile + 1]),
                mark=(("eid4", c) if a == TPC - 1 else None))

        P.wtm("scalar", ("nmx4", c))
        P.wtm("scalar", ("slots4", c - 1), required=False)
        for a in range(TPC):
            P.do("scalar", lambda q, a=a: q.activation(
                out=ex_s[:], in_=lg4_s[:, a, :], func=AF.Exp, bias=nmx4_s[:, a:a + 1],
                accum_out=sm4_s[:, a:a + 1]),
                mark=(("gexp4", c) if a == TPC - 1 else None))

        P.wtm("tensor", ("oh4", c))
        for a in range(TPC):
            P.do("tensor", lambda q, a=a: q.matmul(
                prt[:, 4 + a, :], tri_s[:], ohb_s[:, a, :], start=True, stop=True))
            P.do("tensor", lambda q, a=a: q.matmul(
                prt[0:1, 12 + a, :], onescol_bf[:], ohb_s[:, a, :],
                start=True, stop=True),
                mark=(("rtmm1", c) if a == TPC - 1 else None))

        P.wtm("vector", ("rtmm1", c))
        for a in range(TPC):
            tile = c * TPC + a
            P.do("vector", lambda q, a=a, tile=tile: q.scalar_tensor_tensor(
                out=scr3_s[:, a, :], in0=oh_s[:, a, :], scalar=1.0, in1=prt[:, 4 + a, :],
                op0=OP.mult, op1=OP.mult, accum_out=lrank_s[:, tile:tile + 1]))
        P.do("vector", lambda q: q.tensor_copy(basesrow_s[0:1, 0:E], runb_s[:]))
        for a in range(1, TPC):
            P.do("vector", lambda q, a=a: q.tensor_tensor(
                out=basesrow_s[0:1, a * E:(a + 1) * E],
                in0=basesrow_s[0:1, (a - 1) * E:a * E],
                in1=prt[0:1, 12 + (a - 1), :], op=OP.add))
        P.do("vector", lambda q: q.tensor_tensor(
            out=runb_s[:], in0=basesrow_s[0:1, (TPC - 1) * E:TPC * E],
            in1=prt[0:1, 12 + TPC - 1, :], op=OP.add), mark=("bases", c))

        P.wtm("tensor", ("bases", c))
        P.do("tensor", lambda q: q.matmul(
            prt[:, 8:12, :].rearrange("p a e -> p (a e)"), onesrow_s[:],
            basesrow_s[:], start=True, stop=True), mark=("rtmm2", c))

        P.wtm("vector", ("rtmm2", c))
        P.wtm("vector", ("gexp4", c))
        for a in range(TPC):
            tile = c * TPC + a
            P.do("vector", lambda q, a=a, tile=tile: q.scalar_tensor_tensor(
                out=scr3_s[:, a, :], in0=oh_s[:, a, :], scalar=1.0, in1=prt[:, 8 + a, :],
                op0=OP.mult, op1=OP.mult, accum_out=be_s[:, 0:1]))
            P.do("vector", lambda q, a=a, tile=tile: q.tensor_tensor(
                out=lrank_s[:, tile:tile + 1], in0=lrank_s[:, tile:tile + 1],
                in1=be_s[:], op=OP.add))
        t0 = c * TPC
        sl4 = slots_s[:, t0:t0 + TPC]
        lr4 = lrank_s[:, t0:t0 + TPC]
        P.do("vector", lambda q, t0=t0, lr4=lr4: q.tensor_scalar(
            out=lr4, in0=lr4, scalar1=-1.0, scalar2=None, op0=OP.add))
        P.do("vector", lambda q, t0=t0: q.reciprocal(
            gate_s[:, t0:t0 + TPC], sm4_s[:]))
        P.do("vector", lambda q, t0=t0, sl4=sl4: q.tensor_scalar(
            out=sl4, in0=eid_s[:, t0:t0 + TPC], scalar1=float(CL), scalar2=None,
            op0=OP.mult))
        P.do("vector", lambda q, sl4=sl4, lr4=lr4: q.tensor_tensor(
            out=sl4, in0=sl4, in1=lr4, op=OP.add))
        P.do("vector", lambda q, lr4=lr4: q.tensor_scalar(
            out=acc4_s[:], in0=lr4, scalar1=float(CL), scalar2=None, op0=OP.is_lt))
        P.do("vector", lambda q, sl4=sl4: q.tensor_scalar(
            out=sl4, in0=sl4, scalar1=float(DUMP), scalar2=None, op0=OP.subtract))
        P.do("vector", lambda q, sl4=sl4: q.tensor_tensor(
            out=sl4, in0=sl4, in1=acc4_s[:], op=OP.mult))
        P.do("vector", lambda q, sl4=sl4: q.tensor_scalar(
            out=sl4, in0=sl4, scalar1=float(DUMP), scalar2=None, op0=OP.add))
        P.do("vector", lambda q, t0=t0, sl4=sl4: q.tensor_copy(
            slotsi_s[:, t0:t0 + TPC], sl4), mark=("slots4", c))

        P.wtm("gpsimd", ("slots4", c))
        P.wtm("gpsimd", ("x1w", c))
        x1b = x1b4[c % 2]
        for a in range(TPC):
            tile = c * TPC + a
            P.dma("gpsimd", lambda q, tile=tile, a=a, x1b=x1b: q.indirect_dma_start(
                out=bufq,
                out_offset=bass.IndirectOffsetOnAxis(ap=slotsi_s[:, tile:tile + 1], axis=0),
                in_=x1b[:, a, :], in_offset=None), sSCT[c % 2],
                mark=("scat", c, a))

    p0join = [P.selfsem["vector"], P.selfsem["scalar"], P.selfsem["tensor"],
              P.selfsem["gpsimd"], sLC, sCast]
    for _e in ENG:
        P.join(_e, p0join)
    xfb_load(0)
    kvr_wtile_load(0, 0)
    for c in range(NCH):
        emit_A(c)
        if c >= 2:
            emit_router_tensor1(c - 2)
            emit_router_rest(c - 2)
        if c >= 1:
            emit_B(c - 1)
        if c + 1 < NCH:
            xfb_load(c + 1)
    emit_router_tensor1(NCH - 2)
    emit_router_rest(NCH - 2)
    emit_B(NCH - 1)
    emit_router_tensor1(NCH - 1)
    emit_router_rest(NCH - 1)
    actx.close()

    bctx = contextlib.ExitStack()
    bsb = lambda n, s, d=F32: bctx.enter_context(nc.sbuf_tensor(n, s, d))
    wgt_s = [bsb(f"wgt_s{i}", [128, G, D], BF16) for i in range(2)]
    wut_s = [bsb(f"wut_s{i}", [128, G, D], BF16) for i in range(2)]
    wdt_s = [bsb(f"wdt_s{i}", [128, G, D], BF16) for i in range(2)]
    buf_s = [bsb(f"buf_s{i}", [128, G, CL], BF16) for i in range(2)]
    h_s = bsb("h_s", [128, G, CL], BF16)
    sg_s = bsb("sg_s", [128, CL], BF16)
    yst_s = [bsb(f"yst_s{i}", [128, D]) for i in range(3)]
    zrow_s = bsb("zrow_s", [1, D])
    pg4 = pk4[0][:, 0:CL]
    pu4 = pk4[1][:, 0:CL]
    py4 = [po[0], po[1]]

    alljoin = [P.selfsem[e] for e in ("vector", "scalar", "gpsimd", "tensor")] + \
        sXT + sXF + sSP + sSCT + sWT
    for e in ENG:
        P.join(e, alljoin)

    P.do("vector", lambda q: q.memset(zrow_s[:], 0.0), mark="zrow")
    P.wtm("gpsimd", "zrow")
    P.dma("gpsimd", lambda q: q.dma_start(out=ybuf[DUMP:DUMP + 1, :], in_=zrow_s[:]),
          sYW[0])

    def p4_weights(e):
        slot = e % 2
        if e >= 2:
            P.wtm("gpsimd", ("ymm", e - 2, STILE - 1, 1))
            P.wtm("gpsimd", ("umm", e - 2, G - 1))
        for dst, src in ((wgt_s[slot], wg), (wut_s[slot], wu), (wdt_s[slot], wd)):
            P.dma("gpsimd", lambda q, d=dst, s=src, e=e: q.dma_start(
                out=d[:], in_=s[e].rearrange("(g p) f -> p g f", p=128)), sWE[slot])
        P.mark(("we", e), sWE[slot])

    def p4_buf(e):
        slot = e % 2
        if e >= 2:
            P.wtm("sync", ("umm", e - 2, G - 1))
        for cg in range(G):
            P.dma("sync", lambda q, e=e, cg=cg, slot=slot: q.dma_start_transpose(
                buf_s[slot][:, cg, :],
                bufq[e * CL:(e + 1) * CL, cg * 128:(cg + 1) * 128]), sBT[slot * 2])
        P.mark(("bt_a", e), sBT[slot * 2])
        P.mark(("bt_b", e), sBT[slot * 2])

    def p4_expert(e):
        slot = e % 2
        P.wtm("tensor", ("we", e))
        P.wtm("tensor", ("bt_a", e))
        P.wtm("tensor", ("bt_b", e))
        for fg in range(G):
            if fg >= 1:
                P.wtm("tensor", ("silu", e, fg - 1))
                P.wtm("tensor", ("hmul", e, fg - 1))
            elif e >= 1:
                P.wtm("tensor", ("silu", e - 1, G - 1))
                P.wtm("tensor", ("hmul", e - 1, G - 1))
                P.wtm("tensor", ("ymm", e - 1, STILE - 1, 1))
            for cg in range(G):
                P.do("tensor", lambda q, fg=fg, cg=cg, slot=slot: q.matmul(
                    pg4, wgt_s[slot][:, cg, fg * 128:(fg + 1) * 128],
                    buf_s[slot][:, cg, :], start=(cg == 0), stop=(cg == G - 1)),
                    mark=(("gmm", e, fg) if cg == G - 1 else None))
            for cg in range(G):
                P.do("tensor", lambda q, fg=fg, cg=cg, slot=slot: q.matmul(
                    pu4, wut_s[slot][:, cg, fg * 128:(fg + 1) * 128],
                    buf_s[slot][:, cg, :], start=(cg == 0), stop=(cg == G - 1)),
                    mark=(("umm", e, fg) if cg == G - 1 else None))
            P.wtm("scalar", ("gmm", e, fg))
            if fg >= 1:
                P.wtm("scalar", ("hmul", e, fg - 1))
            elif e >= 1:
                P.wtm("scalar", ("hmul", e - 1, G - 1))
            P.do("scalar", lambda q: q.activation(out=sg_s[:], in_=pg4, func=AF.Silu),
                 mark=("silu", e, fg))
            P.wtm("vector", ("umm", e, fg))
            P.wtm("vector", ("silu", e, fg))
            if e >= 1 and fg == 0:
                P.wtm("vector", ("ymm", e - 1, STILE - 1, 1))
            P.do("vector", lambda q, fg=fg: q.tensor_tensor(
                out=h_s[:, fg, :], in0=sg_s[:], in1=pu4, op=OP.mult),
                mark=("hmul", e, fg))

        for st in range(STILE):
            kk = e * STILE + st
            for h in range(2):
                pb = py4[h]
                if ("yev", kk - 1) in marks:
                    P.wtm("tensor", ("yev", kk - 1))
                P.wtm("tensor", ("hmul", e, G - 1))
                for Fg in range(G):
                    P.do("tensor", lambda q, st=st, h=h, Fg=Fg, pb=pb, slot=slot: q.matmul(
                        pb[:], h_s[:, Fg, st * 128:(st + 1) * 128],
                        wdt_s[slot][:, Fg, h * CH:(h + 1) * CH],
                        start=(Fg == 0), stop=(Fg == G - 1)),
                        mark=(("ymm", e, st, h) if Fg == G - 1 else None))
            yslot = kk % 3
            if ("ystk", kk - 3) in marks:
                P.wtm("vector", ("ystk", kk - 3))
            for h in range(2):
                P.wtm("vector", ("ymm", e, st, h))
                P.do("vector", lambda q, h=h, yslot=yslot: q.tensor_copy(
                    yst_s[yslot][:, h * CH:(h + 1) * CH], py4[h][:]),
                    mark=(("yev", kk) if h == 1 else None))
            P.wtm("gpsimd", ("yev", kk))
            P.dma("gpsimd", lambda q, e=e, st=st, yslot=yslot: q.dma_start(
                out=ybuf[e * CL + st * 128:e * CL + (st + 1) * 128, :],
                in_=yst_s[yslot][:]), sYW[yslot], mark=("ystk", kk))

    p4_weights(0)
    p4_weights(1)
    for _s in sSCT:
        P.wt("sync", _s, _s.n)
    for e in range(E):
        p4_buf(e)
        p4_expert(e)
        if e + 2 < E:
            p4_weights(e + 2)

    alljoin2 = [P.selfsem[e] for e in ("vector", "scalar", "gpsimd", "tensor")] + \
        sWE + sBT + sYW
    for e in ENG:
        P.join(e, alljoin2)
    bctx.close()
    b5ctx = contextlib.ExitStack()
    bsb = lambda n, s, d=F32: b5ctx.enter_context(nc.sbuf_tensor(n, s, d))
    y_s = [bsb(f"y_s{i}", [128, TPC, D]) for i in range(2)]
    spld_s = [bsb(f"spld_s{i}", [128, TPC, D]) for i in range(2)]
    z24_s = bsb("z24_s", [128, TPC, D])
    o4_s = [bsb(f"o4_s{i}", [128, TPC, D]) for i in range(2)]
    z2q_s = bsb("z2q_s", [128, TPC])
    z2p_s = bsb("z2p_s", [128, TPC])
    p5mu_s = bsb("p5mu_s", [128, TPC])
    p5rs_s = bsb("p5rs_s", [128, TPC])
    p5nm_s = bsb("p5nm_s", [128, TPC])
    p5t_s = bsb("p5t_s", [128, TPC])
    p5v_s = bsb("p5v_s", [128, TPC])

    for _s in sYW:
        P.wt("gpsimd", _s, _s.n)
        P.wt("sync", _s, _s.n)

    def p5_chunk(c):
        cs = c % 2
        if c >= 2:
            P.wtm("gpsimd", ("p5z", c - 2))
            P.wtm("sync", ("p5z", c - 2))
        for a in range(TPC):
            tile = c * TPC + a
            P.dma("gpsimd", lambda q, tile=tile, a=a, cs=cs: q.indirect_dma_start(
                out=y_s[cs][:, a, :], out_offset=None, in_=ybuf,
                in_offset=bass.IndirectOffsetOnAxis(ap=slotsi_s[:, tile:tile + 1], axis=0)),
                sYG[cs], mark=("p5g", c))
        P.dma("sync", lambda q, c=c, cs=cs: q.dma_start(
            out=spld_s[cs][:],
            in_=x1sp[c * CH:(c + 1) * CH, :].rearrange("(a p) d -> p a d", p=128)),
            sYL[cs], mark=("p5l", c))
        if c >= 1:
            P.wtm("vector", ("p5o", c - 1))
        P.wtm("vector", ("p5g", c))
        P.wtm("vector", ("p5l", c))
        for a in range(TPC):
            tile = c * TPC + a
            P.do("vector", lambda q, tile=tile, a=a, cs=cs: q.scalar_tensor_tensor(
                out=z24_s[:, a, :], in0=y_s[cs][:, a, :],
                scalar=gate_s[:, tile:tile + 1], in1=spld_s[cs][:, a, :],
                op0=OP.mult, op1=OP.add, accum_out=z2p_s[:, a:a + 1]),
                mark=(("p5z", c) if a == TPC - 1 else None))
        for a in range(TPC):
            P.wtm("scalar", ("p5z", c))
            P.do("scalar", lambda q, a=a: q.activation(
                out=zsq_s[:], in_=z24_s[:, a, :], func=AF.Square,
                accum_out=z2q_s[:, a:a + 1]), mark=(("p5q", c) if a == TPC - 1 else None))
        P.wtm("vector", ("p5q", c))
        P.do("vector", lambda q: q.tensor_scalar(
            out=p5mu_s[:], in0=z2p_s[:], scalar1=1.0 / D, scalar2=None, op0=OP.mult))
        P.do("vector", lambda q: q.tensor_scalar(
            out=p5t_s[:], in0=z2q_s[:], scalar1=1.0 / D, scalar2=None, op0=OP.mult))
        P.do("vector", lambda q: q.tensor_tensor(
            out=p5v_s[:], in0=p5mu_s[:], in1=p5mu_s[:], op=OP.mult))
        P.do("vector", lambda q: q.tensor_tensor(
            out=p5t_s[:], in0=p5t_s[:], in1=p5v_s[:], op=OP.subtract),
            mark=("p5v", c))
        P.wtm("scalar", ("p5v", c))
        P.do("scalar", lambda q: q.activation(
            out=p5rs_s[:], in_=p5t_s[:], func=AF.Ln, bias=epscol_s[:, 0:1]))
        P.do("scalar", lambda q: q.activation(
            out=p5rs_s[:], in_=p5rs_s[:], func=AF.Exp, scale=-0.5), mark=("p5r", c))
        P.wtm("vector", ("p5r", c))
        P.do("vector", lambda q: q.tensor_tensor(
            out=p5nm_s[:], in0=p5mu_s[:], in1=p5rs_s[:], op=OP.mult))
        P.do("vector", lambda q: q.tensor_scalar(
            out=p5nm_s[:], in0=p5nm_s[:], scalar1=-1.0, scalar2=None, op0=OP.mult),
            mark=("p5c", c))
        oslot = c % 2
        if c >= 2:
            P.wtm("scalar", ("outw", c - 2))
        P.wtm("scalar", ("p5c", c))
        for a in range(TPC):
            P.do("scalar", lambda q, a=a, oslot=oslot: q.activation(
                out=o4_s[oslot][:, a, :], in_=z24_s[:, a, :], func=AF.Identity,
                scale=p5rs_s[:, a:a + 1], bias=p5nm_s[:, a:a + 1]),
                mark=(("p5o", c) if a == TPC - 1 else None))
        P.wtm("sync", ("p5o", c))
        P.dma("sync", lambda q, c=c, oslot=oslot: q.dma_start(
            out=out[c * CH:(c + 1) * CH, :].rearrange("(a p) d -> p a d", p=128),
            in_=o4_s[oslot][:]), sOut[oslot], mark=("outw", c))

    for c in range(NCH):
        p5_chunk(c)

    if debug:
        dbgr_s = bsb("dbgr_s", [128, NTILE, 4])
        for _s in sSP:
            P.wt("sync", _s, _s.n)
        P.dma("sync", lambda q: q.dma_start(out=dram["dbg_x1"], in_=x1sp), sOut[0])
        P.join("vector", [P.selfsem["vector"]])
        P.do("vector", lambda q: q.tensor_copy(dbgr_s[:, :, 0:1], eid_s[:].unsqueeze(2)))
        P.do("vector", lambda q: q.tensor_copy(dbgr_s[:, :, 1:2], lrank_s[:].unsqueeze(2)))
        P.do("vector", lambda q: q.tensor_copy(dbgr_s[:, :, 2:3], gate_s[:].unsqueeze(2)))
        P.do("vector", lambda q: q.tensor_copy(dbgr_s[:, :, 3:4], slots_s[:].unsqueeze(2)),
             mark="dbgr")
        P.wtm("sync", "dbgr")
        P.dma("sync", lambda q: q.dma_start(
            out=dram["dbg_route"].rearrange("(a p) f -> p a f", p=128),
            in_=dbgr_s[:]), sOut[0])
        P.dma("sync", lambda q: q.dma_start(out=dram["dbg_cnt"], in_=runb_s[:]), sOut[1])

    for _s in sOut:
        P.wt("sync", _s, _s.n)
    b5ctx.close()

    with octx:
        with nc.Block() as block:
            P.finalize(block)
    return nc


def _shard_inputs(inputs):
    x = np.asarray(inputs["L0_attn_x"], np.float32)
    Wk = np.asarray(inputs["L0_attn_Wk"], np.float32)
    Wv = np.asarray(inputs["L0_attn_Wv"], np.float32)
    Wr = np.asarray(inputs["L0_attn_Wr"], np.float32)
    Wo = np.ascontiguousarray(np.asarray(inputs["L0_attn_Wo"], np.float32))
    wrout = np.ascontiguousarray(np.asarray(inputs["L0_ffn_w_router"], np.float32))
    wgv = np.ascontiguousarray(np.asarray(inputs["L0_ffn_w_gate"], np.float32))
    wuv = np.ascontiguousarray(np.asarray(inputs["L0_ffn_w_up"], np.float32))
    wdv = np.ascontiguousarray(np.asarray(inputs["L0_ffn_w_down"], np.float32))
    pp = lambda v: np.ascontiguousarray(np.asarray(v, np.float32).reshape(G, 128).T)
    shared = {
        "wkT": np.ascontiguousarray(Wk.T), "wvT": np.ascontiguousarray(Wv.T),
        "wrT": np.ascontiguousarray(Wr.T), "woT": np.ascontiguousarray(Wo.T),
        "wo_n": Wo, "wrout": wrout, "wg": wgv, "wu": wuv, "wd": wdv,
        "tmk": pp(inputs["L0_attn_time_mix_k"]), "tmv": pp(inputs["L0_attn_time_mix_v"]),
        "tmr": pp(inputs["L0_attn_time_mix_r"]), "uu": pp(inputs["L0_attn_u"]),
        "ww": pp(inputs["L0_attn_w"]),
        "iotab": np.ascontiguousarray(
            np.broadcast_to(np.arange(E, dtype=np.float32), (128, E))),
        "identf": np.eye(128, dtype=np.float32),
        "trib": np.ascontiguousarray(np.triu(np.ones((128, 128), np.float32))),
        "rstmk": np.ascontiguousarray(np.broadcast_to(
            np.tile((np.arange(E) != 0).astype(np.float32), TPC), (128, TPC * E))),
    }
    in_maps = []
    for core in range(NCORE):
        xb = x[core * COREB:(core + 1) * COREB]
        m = dict(shared)
        m["xf"] = np.ascontiguousarray(xb.transpose(0, 2, 1))
        m["xt"] = np.ascontiguousarray(xb.reshape(TOK, D))
        in_maps.append(m)
    return in_maps


def kernel(**inputs):
    in_maps = _shard_inputs(inputs)
    nc = build_program(debug=False)
    res = run_bass_kernel_spmd(nc, in_maps, core_ids=list(range(NCORE)))
    outs = [np.asarray(r["out"], np.float32) for r in res.results]
    return np.concatenate(outs, axis=0).reshape(32, 2048, 1024)


# version 06
# speedup vs baseline: 1.1362x; 1.0960x over previous
import contextlib
import numpy as np
import concourse.bass as bass
import concourse.mybir as mybir
from concourse.bass_utils import run_bass_kernel_spmd
from concourse.alu_op_type import AluOpType as OP

F32 = mybir.dt.float32
BF16 = mybir.dt.bfloat16
I32 = mybir.dt.int32
AF = mybir.ActivationFunctionType

NCORE = 8
D = 1024
G = 8
E = 32
COREB = 4
T = 2048
TOK = COREB * T
CH = 512
NCH = TOK // CH
TPC = CH // 128
NTILE = TOK // 128
CL = 384
DUMP = E * CL
STILE = CL // 128
EPS = 1e-5

ENG = ("sync", "tensor", "vector", "scalar", "gpsimd")


class SemC:
    __slots__ = ("h", "n", "name")

    def __init__(self, h, name):
        self.h = h
        self.n = 0
        self.name = name


class Prog:
    CHAIN = ("vector", "scalar", "gpsimd")

    def __init__(self, nc, ctx):
        self.nc = nc
        self.ctx = ctx
        self.q = {e: [] for e in ENG}
        self.marks = {}
        self.selfsem = {}

    def init_selfsems(self):
        for e in self.CHAIN + ("tensor",):
            self.selfsem[e] = self.sem("self_" + e)

    def sem(self, name):
        return SemC(self.ctx.enter_context(self.nc.semaphore(name)), name)

    def do(self, eng, fn, mark=None, inc=None, n=1, dep=True):
        if eng in self.CHAIN:
            s = self.selfsem[eng]
            if dep:
                self.q[eng].append(lambda q: q.drain())
            if mark is not None:
                s.n += 1
                self.q[eng].append(lambda q, fn=fn, h=s.h: fn(q).then_inc(h, 1))
                self.marks[mark] = (s, s.n)
            else:
                self.q[eng].append(fn)
        elif eng == "tensor":
            s = self.selfsem[eng]
            if mark is not None:
                s.n += 1
                self.q[eng].append(lambda q, fn=fn, h=s.h: fn(q).then_inc(h, 1))
                self.marks[mark] = (s, s.n)
            else:
                self.q[eng].append(fn)
        else:
            raise ValueError(eng)

    def dma(self, eng, fn, sem, mark=None):
        sem.n += 16
        self.q[eng].append(lambda q, fn=fn, s=sem.h: fn(q).then_inc(s, 16))
        if mark is not None:
            self.marks[mark] = (sem, sem.n)

    def wt(self, eng, sem, val=None):
        v = sem.n if val is None else val
        if v > 0:
            self.q[eng].append(lambda q, s=sem.h, v=v: q.wait_ge(s, v))

    def mark(self, key, x):
        s = self.selfsem[x] if isinstance(x, str) else x
        self.marks[key] = (s, s.n)

    def wtm(self, eng, key, required=True):
        if key not in self.marks:
            assert not required, f"missing mark {key}"
            return
        s, v = self.marks[key]
        self.wt(eng, s, v)

    def join(self, eng, sems):
        for s in sems:
            self.wt(eng, s, s.n)

    def finalize(self, block):
        for e in ENG:
            ops = self.q[e]

            def body(q, ops=ops):
                for f in ops:
                    f(q)

            getattr(block, e)(body)


def build_program(debug=False):
    nc = bass.Bass("TRN2", target_bir_lowering=False)
    dram = {}

    def din(name, shape, dt=F32):
        dram[name] = nc.dram_tensor(name, shape, dt, kind="ExternalInput").ap()
        return dram[name]

    def dout(name, shape, dt=F32):
        dram[name] = nc.dram_tensor(name, shape, dt, kind="ExternalOutput").ap()
        return dram[name]

    def dint(name, shape, dt=F32):
        dram[name] = nc.dram_tensor(name, shape, dt, kind="Internal").ap()
        return dram[name]

    xf = din("xf", [COREB, D, T])
    xt = din("xt", [TOK, D])
    wkT = din("wkT", [D, D])
    wvT = din("wvT", [D, D])
    wrT = din("wrT", [D, D])
    woT = din("woT", [D, D])
    wo_n = din("wo_n", [D, D])
    wrout = din("wrout", [D, E])
    wg = din("wg", [E, D, D])
    wu = din("wu", [E, D, D])
    wd = din("wd", [E, D, D])
    tmk = din("tmk", [128, G])
    tmv = din("tmv", [128, G])
    tmr = din("tmr", [128, G])
    uu = din("uu", [128, G])
    ww = din("ww", [128, G])
    iotab = din("iotab", [128, E])
    identf = din("identf", [128, 128])
    trib = din("trib", [128, 128])
    rstmk = din("rstmk", [128, TPC * E])

    out = dout("out", [TOK, D])
    if debug:
        dout("dbg_x1", [TOK, D])
        dout("dbg_route", [NTILE * 128, 4])
        dout("dbg_cnt", [1, E])

    wkbf = dint("wkbf", [D, D], BF16)
    wvbf = dint("wvbf", [D, D], BF16)
    wrbf = dint("wrbf", [D, D], BF16)
    bufq = dint("bufq", [DUMP + 1, D], BF16)
    ybuf = dint("ybuf", [DUMP + 1, D])
    x1sp = dint("x1sp", [TOK, D])

    octx = contextlib.ExitStack()
    P = Prog(nc, octx)
    marks = P.marks
    osb = lambda n, s, d=F32: octx.enter_context(nc.sbuf_tensor(n, s, d))
    ops_ = lambda n, s, d=F32: octx.enter_context(nc.psum_tensor(n, s, d))

    tmk_s = osb("tmk_s", [128, G])
    tmv_s = osb("tmv_s", [128, G])
    tmr_s = osb("tmr_s", [128, G])
    uu_s = osb("uu_s", [128, G])
    ww_s = osb("ww_s", [128, G])
    ewn_s = osb("ewn_s", [128, G])
    eu_s = osb("eu_s", [128, G])
    iota_s = osb("iota_s", [128, E])
    ident_s = osb("ident_s", [128, 128])
    tri_s = osb("tri_s", [128, 128], BF16)
    rstm_s = osb("rstm_s", [128, TPC, E])
    onescol_bf = osb("onescol_bf", [128, 1], BF16)
    onesrow_s = osb("onesrow_s", [1, 128])
    onecol_s = osb("onecol_s", [128, 1])
    epscol_s = osb("epscol_s", [128, 1])
    cs_s = osb("cs_s", [128, E])
    wr_s = osb("wr_s", [128, G, E], BF16)
    m_s = osb("m_s", [128, G, E], BF16)
    wo_s = osb("wo_s", [128, G, D], BF16)
    runb_s = osb("runb_s", [1, E])
    basesrow_s = osb("basesrow_s", [1, TPC * E])
    eid_s = osb("eid_s", [128, NTILE])
    lrank_s = osb("lrank_s", [128, NTILE])
    gate_s = osb("gate_s", [128, NTILE])
    slots_s = osb("slots_s", [128, NTILE])
    slotsi_s = osb("slotsi_s", [128, NTILE], I32)
    mu4_s = [osb(f"mu4_s{i}", [128, TPC]) for i in range(2)]
    rstd4_s = [osb(f"rstd4_s{i}", [128, TPC]) for i in range(2)]
    acc4_s = osb("acc4_s", [128, TPC])
    rstd24_s = osb("rstd24_s", [128, TPC])
    nmr4_s = osb("nmr4_s", [128, TPC])
    nmr24_s = osb("nmr24_s", [128, TPC])
    tt4_s = osb("tt4_s", [128, TPC])
    vv4_s = osb("vv4_s", [128, TPC])
    zsq_s = osb("zsq_s", [128, D], BF16)
    zps_s = osb("zps_s", [128, TPC, 2])
    z2s_s = osb("z2s_s", [128, TPC])
    be_s = osb("be_s", [128, 1])
    sm4_s = osb("sm4_s", [128, TPC])

    pk4 = [ops_(f"pk4_{i}", [128, CH]) for i in range(4)]
    po = [ops_(f"po{i}", [128, CH]) for i in range(2)]
    plg = ops_("plg", [32, CH])
    prt = ops_("prt", [128, 16, E])

    P.init_selfsems()
    sT = P.selfsem["tensor"]
    sLC = P.sem("sLC")
    sCast = P.sem("sCast")
    sXF = [P.sem(f"sXF{i}") for i in range(2)]
    sXT = [P.sem(f"sXT{i}") for i in range(2)]
    sWT = [P.sem(f"sWT{i}") for i in range(2)]
    sSP = [P.sem(f"sSP{i}") for i in range(2)]
    sSCT = [P.sem(f"sSCT{i}") for i in range(2)]
    sWE = [P.sem(f"sWE{i}") for i in range(2)]
    sBT = [P.sem(f"sBT{i}") for i in range(4)]
    sYW = [P.sem(f"sYW{i}") for i in range(3)]
    sYG = [P.sem(f"sYG{i}") for i in range(3)]
    sYL = [P.sem(f"sYL{i}") for i in range(3)]
    sOut = [P.sem(f"sOut{i}") for i in range(2)]

    p0ctx = contextlib.ExitStack()
    psb = lambda n, s, d=F32: p0ctx.enter_context(nc.sbuf_tensor(n, s, d))
    wof = psb("wof", [128, G, D], BF16)
    wrf = psb("wrf", [128, G, E], BF16)
    wrf32 = psb("wrf32", [128, G, E])
    csrow = psb("csrow", [1, E])

    for dst, src in ((tmk_s, tmk), (tmv_s, tmv), (tmr_s, tmr), (uu_s, uu),
                     (ww_s, ww), (iota_s, iotab), (ident_s, identf)):
        P.dma("sync", lambda q, d=dst, s=src: q.dma_start(out=d[:], in_=s), sLC)
    P.dma("sync", lambda q: q.dma_start(
        out=wrf32[:], in_=wrout.rearrange("(g p) e -> p g e", p=128)), sLC)
    P.dma("sync", lambda q: q.dma_start(
        out=rstm_s[:], in_=rstmk.rearrange("p (a e) -> p a e", e=E)), sLC)
    P.mark("p0_ld", sLC)

    P.dma("gpsimd", lambda q: q.dma_start(out=wkbf, in_=wkT), sCast)
    P.dma("gpsimd", lambda q: q.dma_start(out=wvbf, in_=wvT), sCast)
    P.dma("gpsimd", lambda q: q.dma_start(out=wrbf, in_=wrT), sCast)
    P.dma("gpsimd", lambda q: q.dma_start(out=tri_s[:], in_=trib), sCast)
    P.dma("gpsimd", lambda q: q.dma_start(
        out=wo_s[:], in_=woT.rearrange("(g p) d -> p g d", p=128)), sCast)
    P.dma("gpsimd", lambda q: q.dma_start(
        out=wr_s[:], in_=wrout.rearrange("(g p) e -> p g e", p=128)), sCast)
    P.dma("gpsimd", lambda q: q.dma_start(
        out=wof[:], in_=wo_n.rearrange("(g p) c -> p g c", p=128)), sCast)
    P.dma("gpsimd", lambda q: q.dma_start(
        out=wrf[:], in_=wrout.rearrange("(g p) e -> p g e", p=128)), sCast)
    P.mark("p0_casts", sCast)

    P.do("vector", lambda q: q.memset(onesrow_s[:], 1.0))
    P.do("vector", lambda q: q.memset(onecol_s[:], 1.0))
    P.do("vector", lambda q: q.memset(epscol_s[:], EPS))
    P.do("vector", lambda q: q.memset(runb_s[:], 0.0))
    P.do("vector", lambda q: q.memset(onescol_bf[:], 1.0), mark="p0_memset")

    P.wtm("scalar", "p0_ld")
    P.do("scalar", lambda q: q.activation(out=ewn_s[:], in_=ww_s[:], func=AF.Exp))
    P.do("scalar", lambda q: q.activation(out=ewn_s[:], in_=ewn_s[:], func=AF.Exp,
                                          scale=-1.0))
    P.do("scalar", lambda q: q.activation(out=eu_s[:], in_=uu_s[:], func=AF.Exp),
         mark="p0_scal")

    P.wtm("tensor", "p0_casts")
    P.wtm("tensor", "p0_ld")
    P.wtm("tensor", "p0_memset")
    for cm in range(G):
        for dg in range(G):
            P.do("tensor", lambda q, cm=cm, dg=dg: q.matmul(
                prt[:, 0, :], wof[:, dg, cm * 128:(cm + 1) * 128], wrf[:, dg, :],
                start=(dg == 0), stop=(dg == G - 1)),
                mark=(("p0mm", cm) if dg == G - 1 else None))
        P.wtm("vector", ("p0mm", cm))
        P.do("vector", lambda q, cm=cm: q.tensor_copy(m_s[:, cm, :], prt[:, 0, :]),
             mark=("p0ev", cm))
        P.wtm("tensor", ("p0ev", cm))
    for dg in range(G):
        P.do("tensor", lambda q, dg=dg: q.matmul(
            prt[0:1, 1, :], onecol_s[:], wrf32[:, dg, :],
            start=(dg == 0), stop=(dg == G - 1)),
            mark=("p0cs" if dg == G - 1 else None))
    P.wtm("vector", "p0cs")
    P.do("vector", lambda q: q.tensor_copy(csrow[:], prt[0:1, 1, :]), mark="p0csrow")
    P.wtm("tensor", "p0csrow")
    P.do("tensor", lambda q: q.matmul(prt[:, 2, :], onesrow_s[:], csrow[:],
                                      start=True, stop=True), mark="p0csb")
    P.wtm("vector", "p0csb")
    P.do("vector", lambda q: q.tensor_copy(cs_s[:], prt[:, 2, :]), mark="p0done_v")
    P.wtm("tensor", "p0done_v")
    p0ctx.close()

    actx = contextlib.ExitStack()
    asb = lambda n, s, d=F32: actx.enter_context(nc.sbuf_tensor(n, s, d))
    xfb = [asb(f"xfb{i}", [128, G, CH + 1], BF16) for i in range(2)]
    dx3 = asb("dx3", [128, G, CH], BF16)
    xkb = asb("xkb", [128, G, CH], BF16)
    xvb = asb("xvb", [128, G, CH], BF16)
    xrb = asb("xrb", [128, G, CH], BF16)
    Sbuf = asb("Sbuf", [128, G, CH + 1])
    Dbuf = asb("Dbuf", [128, G, CH + 1])
    NSLOT = 2
    ek_s = [asb(f"ek_s{i}", [128, CH]) for i in range(NSLOT)]
    er_s = [asb(f"er_s{i}", [128, CH]) for i in range(NSLOT)]
    ekv_s = [asb(f"ekv_s{i}", [128, CH]) for i in range(NSLOT)]
    num_s = [asb(f"num_s{i}", [128, CH]) for i in range(NSLOT)]
    den_s = [asb(f"den_s{i}", [128, CH]) for i in range(NSLOT)]
    rwkvb = [asb(f"rwkvb{i}", [128, G, CH], BF16) for i in range(2)]
    wkt_s = [asb(f"wkt_s{i}", [128, G, 128], BF16) for i in range(2)]
    wvt_s = [asb(f"wvt_s{i}", [128, G, 128], BF16) for i in range(2)]
    wrt_s = [asb(f"wrt_s{i}", [128, G, 128], BF16) for i in range(2)]
    xtok4 = asb("xtok4", [128, TPC, D])
    z4 = asb("z4", [128, TPC, D], BF16)
    x1b4 = [asb(f"x1b4_{i}", [128, TPC, D], BF16) for i in range(2)]
    sp_s = [asb(f"sp_s{i}", [128, D]) for i in range(2)]
    lgf_s = asb("lgf_s", [32, CH])
    lg4_s = asb("lg4_s", [128, TPC, E])
    mx4_s = asb("mx4_s", [128, TPC])
    nmx4_s = asb("nmx4_s", [128, TPC])
    ex_s = asb("ex_s", [128, E])
    ohr_s = asb("ohr_s", [128, TPC, E])
    pref_s = asb("pref_s", [128, TPC, E])
    oh_s = asb("oh_s", [128, TPC, E])
    ohb_s = asb("ohb_s", [128, TPC, E], BF16)
    scr3_s = asb("scr3_s", [128, TPC, E])

    def ldidx(c, m):
        return c * G + m

    def kvr_wtile_load(c, m):
        L = ldidx(c, m)
        wb = L % 2
        if L - 2 >= 0:
            P.wtm("scalar", ("proj", 2, (L - 2) // G, (L - 2) % G))
        for dst, src in ((wkt_s[wb], wkbf), (wvt_s[wb], wvbf), (wrt_s[wb], wrbf)):
            P.dma("scalar", lambda q, d=dst, s=src, m=m: q.dma_start(
                out=d[:], in_=s.rearrange("(g p) d -> p g d", p=128)[:, :, m * 128:(m + 1) * 128]),
                sWT[wb])
        P.mark(("wt", c, m), sWT[wb])

    def xfb_load(c):
        b, qh = divmod(c, NCH // COREB)
        t0 = qh * CH
        fb = xfb[c % 2]
        if c >= 2:
            P.wtm("gpsimd", ("mixready", c - 2))
            P.wtm("gpsimd", ("lgx", c - 2))
        if qh == 0:
            P.dma("gpsimd", lambda q, b=b, fb=fb: q.dma_start(
                out=fb[:, :, 1:CH + 1],
                in_=xf[b].rearrange("(g p) t -> p g t", p=128)[:, :, 0:CH]),
                sXF[c % 2], mark=("xf", c))
        else:
            P.dma("gpsimd", lambda q, b=b, t0=t0, fb=fb: q.dma_start(
                out=fb[:, :, :],
                in_=xf[b].rearrange("(g p) t -> p g t", p=128)[:, :, t0 - 1:t0 + CH]),
                sXF[c % 2], mark=("xf", c))

    def emit_A(c):
        b, qh = divmod(c, NCH // COREB)
        fb = xfb[c % 2]
        rw = rwkvb[c % 2]

        P.wtm("vector", ("xf", c))
        if qh == 0:
            P.do("vector", lambda q, fb=fb: q.memset(fb[:, :, 0:1], 0.0))
        if c >= 1:
            P.wtm("vector", ("proj", 2, c - 1, G - 1))
        if c >= 2:
            P.wtm("vector", ("lgmm", c - 2))
        P.do("vector", lambda q, fb=fb: q.tensor_tensor(
            out=dx3[:], in0=fb[:, :, 1:CH + 1], in1=fb[:, :, 0:CH], op=OP.subtract))
        for dst, tm in ((xkb, tmk_s), (xvb, tmv_s), (xrb, tmr_s)):
            P.do("vector", lambda q, dst=dst, tm=tm: q.tensor_tensor(
                out=dst[:], in0=dx3[:],
                in1=tm[:, :].unsqueeze(2).to_broadcast([128, G, CH]), op=OP.mult))
            P.do("vector", lambda q, dst=dst, fb=fb: q.tensor_tensor(
                out=dst[:], in0=dst[:], in1=fb[:, :, 0:CH], op=OP.add),
                mark=(("mixready", c) if dst is xrb else None))

        if c >= 1:
            P.wtm("vector", ("nd", c - 1, G - 1))
        if qh == 0:
            P.do("vector", lambda q: q.memset(Sbuf[:, :, 0:1], 0.0))
            P.do("vector", lambda q: q.memset(Dbuf[:, :, 0:1], 0.0), mark=("carry", c))
        else:
            P.do("vector", lambda q: q.tensor_copy(Sbuf[:, :, 0:1], Sbuf[:, :, CH:CH + 1]))
            P.do("vector", lambda q: q.tensor_copy(Dbuf[:, :, 0:1], Dbuf[:, :, CH:CH + 1]),
                 mark=("carry", c))

        P.wtm("tensor", ("mixready", c))

        def stage_T(m):
            si = ldidx(c, m)
            wb = si % 2
            if m + 1 < G:
                kvr_wtile_load(c, m + 1)
            elif c + 1 < NCH:
                kvr_wtile_load(c + 1, 0)
            P.wtm("tensor", ("wt", c, m))
            for pi, (wts, rhs) in enumerate(((wkt_s[wb], xkb), (wvt_s[wb], xvb),
                                             (wrt_s[wb], xrb))):
                u = si * 3 + pi
                bank = pk4[u % 4]
                if ("pjfree", u - 4) in marks:
                    P.wtm("tensor", ("pjfree", u - 4))
                for gg in range(G):
                    P.do("tensor", lambda q, w=wts, r=rhs, gg=gg, bank=bank: q.matmul(
                        bank[:], w[:, gg, :], r[:, gg, :],
                        start=(gg == 0), stop=(gg == G - 1)),
                        mark=(("proj", pi, c, m) if gg == G - 1 else None))

        def stage_S1(m):
            si = ldidx(c, m)
            slot = m % NSLOT
            kbank = pk4[(si * 3 + 0) % 4]
            rbank = pk4[(si * 3 + 2) % 4]
            prev = ("rwkv", c, m - NSLOT) if m - NSLOT >= 0 else \
                ("rwkv", c - 1, G + m - NSLOT)
            if prev in marks:
                P.wtm("scalar", prev)
            pn = ("nd", c, m - NSLOT) if m - NSLOT >= 0 else ("nd", c - 1, G + m - NSLOT)
            if pn in marks:
                P.wtm("scalar", pn)
            P.wtm("scalar", ("proj", 0, c, m))
            P.do("scalar", lambda q, slot=slot, kbank=kbank: q.activation(
                out=ek_s[slot][:], in_=kbank[:], func=AF.Exp), mark=("ek", c, m),
                dep=False)
            P.mark(("pjfree", si * 3 + 0), P.selfsem["scalar"])
            P.wtm("scalar", ("proj", 2, c, m))
            P.do("scalar", lambda q, slot=slot, rbank=rbank: q.activation(
                out=er_s[slot][:], in_=rbank[:], func=AF.Exp, scale=-1.0),
                mark=("er", c, m), dep=False)
            P.mark(("pjfree", si * 3 + 2), P.selfsem["scalar"])

        def stage_V23(m):
            si = ldidx(c, m)
            slot = m % NSLOT
            vbank = pk4[(si * 3 + 1) % 4]
            pn = ("nd", c, m - NSLOT) if m - NSLOT >= 0 else ("nd", c - 1, G + m - NSLOT)
            if pn in marks:
                P.wtm("vector", pn)
            P.wtm("vector", ("proj", 1, c, m))
            P.wtm("vector", ("ek", c, m))
            P.do("vector", lambda q, slot=slot, vbank=vbank: q.tensor_tensor(
                out=ekv_s[slot][:], in0=ek_s[slot][:], in1=vbank[:], op=OP.mult),
                mark=("ekv", c, m), dep=False)
            P.do("vector", lambda q, slot=slot, m=m: q.tensor_scalar(
                out=num_s[slot][:], in0=ekv_s[slot][:], scalar1=eu_s[:, m:m + 1],
                scalar2=None, op0=OP.mult))
            P.do("vector", lambda q, slot=slot, m=m: q.tensor_scalar(
                out=den_s[slot][:], in0=ek_s[slot][:], scalar1=eu_s[:, m:m + 1],
                scalar2=None, op0=OP.mult), dep=False)
            marks[("pjfree", si * 3 + 1)] = marks[("ekv", c, m)]
            P.wtm("vector", ("carry", c))
            P.do("vector", lambda q, slot=slot, m=m: q.tensor_tensor_scan(
                out=Sbuf[:, m, 1:CH + 1],
                data0=ewn_s[:, m:m + 1].to_broadcast([128, CH]),
                data1=ekv_s[slot][:], initial=Sbuf[:, m, 0:1],
                op0=OP.mult, op1=OP.add), dep=False)
            P.do("vector", lambda q, slot=slot, m=m: q.tensor_tensor_scan(
                out=Dbuf[:, m, 1:CH + 1],
                data0=ewn_s[:, m:m + 1].to_broadcast([128, CH]),
                data1=ek_s[slot][:], initial=Dbuf[:, m, 0:1],
                op0=OP.mult, op1=OP.add), mark=("scan", c, m), dep=False)

        def stage_G4(m):
            slot = m % NSLOT
            P.wtm("gpsimd", ("scan", c, m))
            prev = ("rwkv", c, m - NSLOT) if m - NSLOT >= 0 else \
                ("rwkv", c - 1, G + m - NSLOT)
            if prev in marks:
                P.wtm("gpsimd", prev)
            P.do("gpsimd", lambda q, slot=slot, m=m: q.tensor_tensor(
                out=num_s[slot][:], in0=num_s[slot][:], in1=Sbuf[:, m, 0:CH], op=OP.add))
            P.do("gpsimd", lambda q, slot=slot, m=m: q.tensor_tensor(
                out=den_s[slot][:], in0=den_s[slot][:], in1=Dbuf[:, m, 0:CH], op=OP.add),
                mark=("nd0", c, m), dep=False)

        def stage_VF(m):
            slot = m % NSLOT
            P.wtm("vector", ("er", c, m))
            P.do("vector", lambda q, slot=slot: q.tensor_scalar(
                out=er_s[slot][:], in0=er_s[slot][:], scalar1=1.0, scalar2=None,
                op0=OP.add), dep=False)
            P.wtm("vector", ("nd0", c, m))
            P.do("vector", lambda q, slot=slot: q.tensor_tensor(
                out=den_s[slot][:], in0=den_s[slot][:], in1=er_s[slot][:], op=OP.mult),
                mark=("nd", c, m))

        def stage_S5(m):
            slot = m % NSLOT
            P.wtm("scalar", ("nd", c, m))
            P.do("scalar", lambda q, slot=slot: q.activation(
                out=er_s[slot][:], in_=den_s[slot][:], func=AF.Ln), dep=False)
            P.do("scalar", lambda q, slot=slot: q.activation(
                out=er_s[slot][:], in_=er_s[slot][:], func=AF.Exp, scale=-1.0),
                mark=("inv", c, m))

        def stage_V6(m):
            slot = m % NSLOT
            P.wtm("gpsimd", ("inv", c, m))
            P.do("gpsimd", lambda q, slot=slot, m=m, rw=rw: q.tensor_tensor(
                out=rw[:, m, :], in0=num_s[slot][:], in1=er_s[slot][:], op=OP.mult),
                mark=("rwkv", c, m))

        for m in range(G):
            if m >= 2:
                stage_V6(m - 2)
            stage_T(m)
            stage_S1(m)
            if m >= 1:
                stage_G4(m - 1)
            stage_V23(m)
            if m >= 1:
                stage_VF(m - 1)
                stage_S5(m - 1)
        stage_G4(G - 1)
        stage_VF(G - 1)
        stage_S5(G - 1)
        stage_V6(G - 2)
        stage_V6(G - 1)
        P.mark(("rwkvall", c), "gpsimd")

    def emit_lgx(c):
        fb = xfb[c % 2]
        P.wtm("tensor", ("xf", c))
        if c >= 1:
            P.wtm("tensor", ("lgev", c - 1))
        for g in range(G):
            P.do("tensor", lambda q, g=g, fb=fb: q.matmul(
                plg[:], wr_s[:, g, :], fb[:, g, 1:CH + 1],
                start=(g == 0), stop=False),
                mark=(("lgx", c) if g == G - 1 else None))

    def emit_B(c):
        b, qh = divmod(c, NCH // COREB)
        fb = xfb[c % 2]
        rw = rwkvb[c % 2]
        if c >= 1:
            P.wtm("sync", ("zt", c - 1, TPC - 1))
        P.dma("sync", lambda q, c=c: q.dma_start(
            out=xtok4[:],
            in_=xt[c * CH:(c + 1) * CH, :].rearrange("(a p) d -> p a d", p=128)),
            sXT[c % 2], mark=("xt4", c))

        P.wtm("tensor", ("rwkvall", c))
        for a in range(TPC):
            tile = c * TPC + a
            for h in range(2):
                oidx = a * 2 + h
                pb = po[oidx % 2]
                pk = ("ouse", c, oidx - 2) if oidx >= 2 else ("ouse", c - 1, 6 + oidx)
                if pk in marks:
                    P.wtm("tensor", pk)
                for g in range(G):
                    P.do("tensor", lambda q, g=g, a=a, h=h, pb=pb, rw=rw: q.matmul(
                        pb[:], rw[:, g, a * 128:(a + 1) * 128],
                        wo_s[:, g, h * CH:(h + 1) * CH],
                        start=(g == 0), stop=(g == G - 1)),
                        mark=(("omm", tile, h) if g == G - 1 else None))
                P.wtm("vector", ("omm", tile, h))
                if a == 0 and h == 0:
                    P.wtm("vector", ("xt4", c))
                    if c >= 1:
                        P.wtm("vector", ("x1w", c - 1))
                P.do("vector", lambda q, a=a, h=h, pb=pb: q.scalar_tensor_tensor(
                    out=z4[:, a, h * CH:(h + 1) * CH], in0=pb[:], scalar=1.0,
                    in1=xtok4[:, a, h * CH:(h + 1) * CH], op0=OP.mult, op1=OP.add,
                    accum_out=zps_s[:, a, h:h + 1]),
                    mark=("ouse", c, oidx), dep=False)
            P.mark(("zt", c, a), P.selfsem["vector"])

        for g in range(G):
            P.do("tensor", lambda q, g=g, rw=rw: q.matmul(
                plg[:], m_s[:, g, :], rw[:, g, :],
                start=False, stop=(g == G - 1)),
                mark=(("lgmm", c) if g == G - 1 else None))

        for a in range(TPC):
            P.wtm("scalar", ("zt", c, a))
            P.do("scalar", lambda q, a=a: q.activation(
                out=zsq_s[:], in_=z4[:, a, :], func=AF.Square,
                accum_out=z2s_s[:, a:a + 1]), mark=("z2", c, a))
        mu4 = mu4_s[c % 2]
        rstd4 = rstd4_s[c % 2]
        P.wtm("vector", ("z2", c, TPC - 1))
        if c >= 2:
            P.wtm("vector", ("gexp4", c - 2))
            P.wtm("vector", ("slots4", c - 2))
        P.do("vector", lambda q, mu4=mu4: q.tensor_tensor(
            out=mu4[:], in0=zps_s[:, :, 0], in1=zps_s[:, :, 1], op=OP.add))
        P.do("vector", lambda q, mu4=mu4: q.tensor_scalar(
            out=mu4[:], in0=mu4[:], scalar1=1.0 / D, scalar2=None, op0=OP.mult))
        P.do("vector", lambda q: q.tensor_scalar(
            out=tt4_s[:], in0=z2s_s[:], scalar1=1.0 / D, scalar2=None, op0=OP.mult))
        P.do("vector", lambda q, mu4=mu4: q.tensor_tensor(
            out=vv4_s[:], in0=mu4[:], in1=mu4[:], op=OP.mult))
        P.do("vector", lambda q: q.tensor_tensor(
            out=tt4_s[:], in0=tt4_s[:], in1=vv4_s[:], op=OP.subtract),
            mark=("var4", c))
        P.wtm("scalar", ("var4", c))
        if c >= 2:
            P.wtm("scalar", ("slots4", c - 2))
        P.do("scalar", lambda q, rstd4=rstd4: q.activation(
            out=rstd4[:], in_=tt4_s[:], func=AF.Ln, bias=epscol_s[:, 0:1]))
        P.do("scalar", lambda q, rstd4=rstd4: q.activation(
            out=rstd4[:], in_=rstd4[:], func=AF.Exp, scale=-0.5), mark=("rstd4", c))
        P.wtm("vector", ("rstd4", c))
        P.do("vector", lambda q, mu4=mu4, rstd4=rstd4: q.tensor_tensor(
            out=nmr4_s[:], in0=mu4[:], in1=rstd4[:], op=OP.mult))
        P.do("vector", lambda q: q.tensor_scalar(
            out=nmr4_s[:], in0=nmr4_s[:], scalar1=-1.0, scalar2=None, op0=OP.mult))
        P.do("vector", lambda q, rstd4=rstd4: q.tensor_scalar(
            out=rstd24_s[:], in0=rstd4[:], scalar1=2.0, scalar2=None, op0=OP.mult))
        P.do("vector", lambda q: q.tensor_scalar(
            out=nmr24_s[:], in0=nmr4_s[:], scalar1=2.0, scalar2=None, op0=OP.mult),
            mark=("lnc4", c))

        x1b = x1b4[c % 2]
        P.wtm("scalar", ("lnc4", c))
        for back in (2, 3):
            if ("scat", c - back, TPC - 1) in marks:
                P.wtm("scalar", ("scat", c - back, TPC - 1))
                break
        for a in range(TPC):
            tile = c * TPC + a
            sps = sp_s[tile % 2]
            if tile >= 2:
                P.wtm("scalar", ("spw", tile - 2))
            P.do("scalar", lambda q, a=a, x1b=x1b, rstd4=rstd4: q.activation(
                out=x1b[:, a, :], in_=z4[:, a, :], func=AF.Identity,
                scale=rstd4[:, a:a + 1], bias=nmr4_s[:, a:a + 1]))
            P.do("scalar", lambda q, a=a, sps=sps: q.activation(
                out=sps[:], in_=z4[:, a, :], func=AF.Identity,
                scale=rstd24_s[:, a:a + 1], bias=nmr24_s[:, a:a + 1]),
                mark=("x1s", tile) if a < TPC - 1 else ("x1w", c))
            P.wtm("sync", ("x1s", tile) if a < TPC - 1 else ("x1w", c))
            P.dma("sync", lambda q, tile=tile, sps=sps: q.dma_start(
                out=x1sp[tile * 128:(tile + 1) * 128, :], in_=sps[:]), sSP[tile % 2],
                mark=("spw", tile))

        P.wtm("vector", ("lgmm", c))
        if c >= 1:
            P.wtm("vector", ("lgtr", c - 1))
        P.do("vector", lambda q: q.tensor_copy(lgf_s[:], plg[:]), mark=("lgev", c))


    def emit_router_tensor1(c):
        P.wtm("tensor", ("lgev", c))
        if c >= 1:
            P.wtm("tensor", ("slots4", c - 1))
        for a in range(TPC):
            P.do("tensor", lambda q, a=a: q.transpose(
                prt[:, a, :], lgf_s[:, a * 128:(a + 1) * 128], ident_s[0:32, 0:32]),
                mark=(("lgtr", c) if a == TPC - 1 else None))

    def emit_router_rest(c):
        P.wtm("vector", ("lgtr", c))
        P.wtm("vector", ("rtmm1", c - 1), required=False)
        P.wtm("vector", ("gexp4", c - 1), required=False)
        mu4 = mu4_s[c % 2]
        rstd4 = rstd4_s[c % 2]
        P.do("vector", lambda q, mu4=mu4: q.tensor_tensor(
            out=scr3_s[:], in0=mu4[:, :].unsqueeze(2).to_broadcast([128, TPC, E]),
            in1=cs_s[:, :].unsqueeze(1).to_broadcast([128, TPC, E]), op=OP.mult))
        P.do("vector", lambda q: q.tensor_tensor(
            out=lg4_s[:], in0=prt[:, 0:TPC, :], in1=scr3_s[:], op=OP.subtract))
        P.do("vector", lambda q, rstd4=rstd4: q.tensor_tensor(
            out=lg4_s[:], in0=lg4_s[:],
            in1=rstd4[:, :].unsqueeze(2).to_broadcast([128, TPC, E]), op=OP.mult),
            mark=("lgaff", c))
        P.do("vector", lambda q: q.tensor_reduce(
            out=mx4_s[:].unsqueeze(2), in_=lg4_s[:], axis=mybir.AxisListType.X,
            op=OP.max))
        P.do("vector", lambda q: q.tensor_scalar(
            out=nmx4_s[:], in0=mx4_s[:], scalar1=-1.0, scalar2=None, op0=OP.mult),
            mark=("nmx4", c))
        P.do("vector", lambda q: q.tensor_tensor(
            out=ohr_s[:], in0=lg4_s[:],
            in1=mx4_s[:, :].unsqueeze(2).to_broadcast([128, TPC, E]), op=OP.is_ge))
        P.do("vector", lambda q: q.tensor_tensor_scan(
            out=pref_s.rearrange("p a e -> p (a e)"), data0=rstm_s.rearrange("p a e -> p (a e)"),
            data1=ohr_s.rearrange("p a e -> p (a e)"), initial=0.0,
            op0=OP.mult, op1=OP.add))
        P.do("vector", lambda q: q.tensor_scalar(
            out=pref_s[:], in0=pref_s[:], scalar1=1.0, scalar2=None, op0=OP.is_equal))
        P.do("vector", lambda q: q.tensor_tensor(
            out=oh_s[:], in0=pref_s[:], in1=ohr_s[:], op=OP.mult))
        P.do("vector", lambda q: q.tensor_copy(ohb_s[:], oh_s[:]), mark=("oh4", c))
        for a in range(TPC):
            tile = c * TPC + a
            P.do("vector", lambda q, a=a, tile=tile: q.scalar_tensor_tensor(
                out=scr3_s[:, a, :], in0=oh_s[:, a, :], scalar=1.0, in1=iota_s[:],
                op0=OP.mult, op1=OP.mult, accum_out=eid_s[:, tile:tile + 1]),
                mark=(("eid4", c) if a == TPC - 1 else None))

        P.wtm("scalar", ("nmx4", c))
        P.wtm("scalar", ("slots4", c - 1), required=False)
        for a in range(TPC):
            P.do("scalar", lambda q, a=a: q.activation(
                out=ex_s[:], in_=lg4_s[:, a, :], func=AF.Exp, bias=nmx4_s[:, a:a + 1],
                accum_out=sm4_s[:, a:a + 1]),
                mark=(("gexp4", c) if a == TPC - 1 else None))

        P.wtm("tensor", ("oh4", c))
        for a in range(TPC):
            P.do("tensor", lambda q, a=a: q.matmul(
                prt[:, 4 + a, :], tri_s[:], ohb_s[:, a, :], start=True, stop=True))
            P.do("tensor", lambda q, a=a: q.matmul(
                prt[0:1, 12 + a, :], onescol_bf[:], ohb_s[:, a, :],
                start=True, stop=True),
                mark=(("rtmm1", c) if a == TPC - 1 else None))

        P.wtm("vector", ("rtmm1", c))
        for a in range(TPC):
            tile = c * TPC + a
            P.do("vector", lambda q, a=a, tile=tile: q.scalar_tensor_tensor(
                out=scr3_s[:, a, :], in0=oh_s[:, a, :], scalar=1.0, in1=prt[:, 4 + a, :],
                op0=OP.mult, op1=OP.mult, accum_out=lrank_s[:, tile:tile + 1]))
        P.do("vector", lambda q: q.tensor_copy(basesrow_s[0:1, 0:E], runb_s[:]))
        for a in range(1, TPC):
            P.do("vector", lambda q, a=a: q.tensor_tensor(
                out=basesrow_s[0:1, a * E:(a + 1) * E],
                in0=basesrow_s[0:1, (a - 1) * E:a * E],
                in1=prt[0:1, 12 + (a - 1), :], op=OP.add))
        P.do("vector", lambda q: q.tensor_tensor(
            out=runb_s[:], in0=basesrow_s[0:1, (TPC - 1) * E:TPC * E],
            in1=prt[0:1, 12 + TPC - 1, :], op=OP.add), mark=("bases", c))

        P.wtm("tensor", ("bases", c))
        P.do("tensor", lambda q: q.matmul(
            prt[:, 8:12, :].rearrange("p a e -> p (a e)"), onesrow_s[:],
            basesrow_s[:], start=True, stop=True), mark=("rtmm2", c))

        P.wtm("vector", ("rtmm2", c))
        P.wtm("vector", ("gexp4", c))
        for a in range(TPC):
            tile = c * TPC + a
            P.do("vector", lambda q, a=a, tile=tile: q.scalar_tensor_tensor(
                out=scr3_s[:, a, :], in0=oh_s[:, a, :], scalar=1.0, in1=prt[:, 8 + a, :],
                op0=OP.mult, op1=OP.mult, accum_out=be_s[:, 0:1]))
            P.do("vector", lambda q, a=a, tile=tile: q.tensor_tensor(
                out=lrank_s[:, tile:tile + 1], in0=lrank_s[:, tile:tile + 1],
                in1=be_s[:], op=OP.add))
        t0 = c * TPC
        sl4 = slots_s[:, t0:t0 + TPC]
        lr4 = lrank_s[:, t0:t0 + TPC]
        P.do("vector", lambda q, t0=t0, lr4=lr4: q.tensor_scalar(
            out=lr4, in0=lr4, scalar1=-1.0, scalar2=None, op0=OP.add))
        P.do("vector", lambda q, t0=t0: q.reciprocal(
            gate_s[:, t0:t0 + TPC], sm4_s[:]))
        P.do("vector", lambda q, t0=t0, sl4=sl4: q.tensor_scalar(
            out=sl4, in0=eid_s[:, t0:t0 + TPC], scalar1=float(CL), scalar2=None,
            op0=OP.mult))
        P.do("vector", lambda q, sl4=sl4, lr4=lr4: q.tensor_tensor(
            out=sl4, in0=sl4, in1=lr4, op=OP.add))
        P.do("vector", lambda q, lr4=lr4: q.tensor_scalar(
            out=acc4_s[:], in0=lr4, scalar1=float(CL), scalar2=None, op0=OP.is_lt))
        P.do("vector", lambda q, sl4=sl4: q.tensor_scalar(
            out=sl4, in0=sl4, scalar1=float(DUMP), scalar2=None, op0=OP.subtract))
        P.do("vector", lambda q, sl4=sl4: q.tensor_tensor(
            out=sl4, in0=sl4, in1=acc4_s[:], op=OP.mult))
        P.do("vector", lambda q, sl4=sl4: q.tensor_scalar(
            out=sl4, in0=sl4, scalar1=float(DUMP), scalar2=None, op0=OP.add))
        P.do("vector", lambda q, t0=t0, sl4=sl4: q.tensor_copy(
            slotsi_s[:, t0:t0 + TPC], sl4), mark=("slots4", c))

        P.wtm("gpsimd", ("slots4", c))
        P.wtm("gpsimd", ("x1w", c))
        x1b = x1b4[c % 2]
        for a in range(TPC):
            tile = c * TPC + a
            P.dma("gpsimd", lambda q, tile=tile, a=a, x1b=x1b: q.indirect_dma_start(
                out=bufq,
                out_offset=bass.IndirectOffsetOnAxis(ap=slotsi_s[:, tile:tile + 1], axis=0),
                in_=x1b[:, a, :], in_offset=None), sSCT[c % 2],
                mark=("scat", c, a))

    p0join = [P.selfsem["vector"], P.selfsem["scalar"], P.selfsem["tensor"],
              P.selfsem["gpsimd"], sLC, sCast]
    for _e in ENG:
        P.join(_e, p0join)
    xfb_load(0)
    kvr_wtile_load(0, 0)
    emit_lgx_done = [-1]
    for c in range(NCH):
        if c + 1 < NCH:
            xfb_load(c + 1)
        emit_A(c)
        if c == 0:
            emit_lgx(0)
        if c >= 2:
            emit_router_tensor1(c - 2)
            emit_router_rest(c - 2)
        if c >= 1:
            emit_B(c - 1)
            emit_lgx(c)
    emit_router_tensor1(NCH - 2)
    emit_router_rest(NCH - 2)
    emit_B(NCH - 1)
    emit_router_tensor1(NCH - 1)
    emit_router_rest(NCH - 1)
    actx.close()

    bctx = contextlib.ExitStack()
    bsb = lambda n, s, d=F32: bctx.enter_context(nc.sbuf_tensor(n, s, d))
    wgt_s = [bsb(f"wgt_s{i}", [128, G, D], BF16) for i in range(2)]
    wut_s = [bsb(f"wut_s{i}", [128, G, D], BF16) for i in range(2)]
    wdt_s = [bsb(f"wdt_s{i}", [128, G, D], BF16) for i in range(2)]
    buf_s = [bsb(f"buf_s{i}", [128, G, CL], BF16) for i in range(2)]
    h_s = bsb("h_s", [128, G, CL], BF16)
    sg_s = bsb("sg_s", [128, CL], BF16)
    yst_s = [bsb(f"yst_s{i}", [128, D]) for i in range(3)]
    zrow_s = bsb("zrow_s", [1, D])
    pg4 = pk4[0][:, 0:CL]
    pu4 = pk4[1][:, 0:CL]
    py4 = [po[0], po[1]]

    alljoin = [P.selfsem[e] for e in ("vector", "scalar", "gpsimd", "tensor")] + \
        sXT + sXF + sSP + sSCT + sWT
    for e in ENG:
        P.join(e, alljoin)

    P.do("vector", lambda q: q.memset(zrow_s[:], 0.0), mark="zrow")
    P.wtm("gpsimd", "zrow")
    P.dma("gpsimd", lambda q: q.dma_start(out=ybuf[DUMP:DUMP + 1, :], in_=zrow_s[:]),
          sYW[0])

    def p4_weights(e):
        slot = e % 2
        if e >= 2:
            P.wtm("gpsimd", ("ymm", e - 2, STILE - 1, 1))
            P.wtm("gpsimd", ("umm", e - 2, G - 1))
        for dst, src in ((wgt_s[slot], wg), (wut_s[slot], wu), (wdt_s[slot], wd)):
            P.dma("gpsimd", lambda q, d=dst, s=src, e=e: q.dma_start(
                out=d[:], in_=s[e].rearrange("(g p) f -> p g f", p=128)), sWE[slot])
        P.mark(("we", e), sWE[slot])

    def p4_buf(e):
        slot = e % 2
        if e >= 2:
            P.wtm("sync", ("umm", e - 2, G - 1))
        for cg in range(G):
            P.dma("sync", lambda q, e=e, cg=cg, slot=slot: q.dma_start_transpose(
                buf_s[slot][:, cg, :],
                bufq[e * CL:(e + 1) * CL, cg * 128:(cg + 1) * 128]), sBT[slot * 2])
        P.mark(("bt_a", e), sBT[slot * 2])
        P.mark(("bt_b", e), sBT[slot * 2])

    def p4_expert(e):
        slot = e % 2
        P.wtm("tensor", ("we", e))
        P.wtm("tensor", ("bt_a", e))
        P.wtm("tensor", ("bt_b", e))
        for fg in range(G):
            if fg >= 1:
                P.wtm("tensor", ("silu", e, fg - 1))
                P.wtm("tensor", ("hmul", e, fg - 1))
            elif e >= 1:
                P.wtm("tensor", ("silu", e - 1, G - 1))
                P.wtm("tensor", ("hmul", e - 1, G - 1))
                P.wtm("tensor", ("ymm", e - 1, STILE - 1, 1))
            for cg in range(G):
                P.do("tensor", lambda q, fg=fg, cg=cg, slot=slot: q.matmul(
                    pg4, wgt_s[slot][:, cg, fg * 128:(fg + 1) * 128],
                    buf_s[slot][:, cg, :], start=(cg == 0), stop=(cg == G - 1)),
                    mark=(("gmm", e, fg) if cg == G - 1 else None))
            for cg in range(G):
                P.do("tensor", lambda q, fg=fg, cg=cg, slot=slot: q.matmul(
                    pu4, wut_s[slot][:, cg, fg * 128:(fg + 1) * 128],
                    buf_s[slot][:, cg, :], start=(cg == 0), stop=(cg == G - 1)),
                    mark=(("umm", e, fg) if cg == G - 1 else None))
            P.wtm("scalar", ("gmm", e, fg))
            if fg >= 1:
                P.wtm("scalar", ("hmul", e, fg - 1))
            elif e >= 1:
                P.wtm("scalar", ("hmul", e - 1, G - 1))
            P.do("scalar", lambda q: q.activation(out=sg_s[:], in_=pg4, func=AF.Silu),
                 mark=("silu", e, fg))
            P.wtm("vector", ("umm", e, fg))
            P.wtm("vector", ("silu", e, fg))
            if e >= 1 and fg == 0:
                P.wtm("vector", ("ymm", e - 1, STILE - 1, 1))
            P.do("vector", lambda q, fg=fg: q.tensor_tensor(
                out=h_s[:, fg, :], in0=sg_s[:], in1=pu4, op=OP.mult),
                mark=("hmul", e, fg))

        for st in range(STILE):
            kk = e * STILE + st
            for h in range(2):
                pb = py4[h]
                if ("yev", kk - 1) in marks:
                    P.wtm("tensor", ("yev", kk - 1))
                P.wtm("tensor", ("hmul", e, G - 1))
                for Fg in range(G):
                    P.do("tensor", lambda q, st=st, h=h, Fg=Fg, pb=pb, slot=slot: q.matmul(
                        pb[:], h_s[:, Fg, st * 128:(st + 1) * 128],
                        wdt_s[slot][:, Fg, h * CH:(h + 1) * CH],
                        start=(Fg == 0), stop=(Fg == G - 1)),
                        mark=(("ymm", e, st, h) if Fg == G - 1 else None))
            yslot = kk % 3
            if ("ystk", kk - 3) in marks:
                P.wtm("vector", ("ystk", kk - 3))
            for h in range(2):
                P.wtm("vector", ("ymm", e, st, h))
                P.do("vector", lambda q, h=h, yslot=yslot: q.tensor_copy(
                    yst_s[yslot][:, h * CH:(h + 1) * CH], py4[h][:]),
                    mark=(("yev", kk) if h == 1 else None))
            P.wtm("gpsimd", ("yev", kk))
            P.dma("gpsimd", lambda q, e=e, st=st, yslot=yslot: q.dma_start(
                out=ybuf[e * CL + st * 128:e * CL + (st + 1) * 128, :],
                in_=yst_s[yslot][:]), sYW[yslot], mark=("ystk", kk))

    p4_weights(0)
    p4_weights(1)
    for _s in sSCT:
        P.wt("sync", _s, _s.n)
    for e in range(E):
        p4_buf(e)
        p4_expert(e)
        if e + 2 < E:
            p4_weights(e + 2)

    alljoin2 = [P.selfsem[e] for e in ("vector", "scalar", "gpsimd", "tensor")] + \
        sWE + sBT + sYW
    for e in ENG:
        P.join(e, alljoin2)
    bctx.close()
    b5ctx = contextlib.ExitStack()
    bsb = lambda n, s, d=F32: b5ctx.enter_context(nc.sbuf_tensor(n, s, d))
    y_s = [bsb(f"y_s{i}", [128, TPC, D]) for i in range(2)]
    spld_s = [bsb(f"spld_s{i}", [128, TPC, D]) for i in range(2)]
    z24_s = bsb("z24_s", [128, TPC, D])
    o4_s = [bsb(f"o4_s{i}", [128, TPC, D]) for i in range(2)]
    z2q_s = bsb("z2q_s", [128, TPC])
    z2p_s = bsb("z2p_s", [128, TPC])
    p5mu_s = bsb("p5mu_s", [128, TPC])
    p5rs_s = bsb("p5rs_s", [128, TPC])
    p5nm_s = bsb("p5nm_s", [128, TPC])
    p5t_s = bsb("p5t_s", [128, TPC])
    p5v_s = bsb("p5v_s", [128, TPC])

    for _s in sYW:
        P.wt("gpsimd", _s, _s.n)
        P.wt("sync", _s, _s.n)

    def p5_chunk(c):
        cs = c % 2
        if c >= 2:
            P.wtm("gpsimd", ("p5z", c - 2))
            P.wtm("sync", ("p5z", c - 2))
        for a in range(TPC):
            tile = c * TPC + a
            P.dma("gpsimd", lambda q, tile=tile, a=a, cs=cs: q.indirect_dma_start(
                out=y_s[cs][:, a, :], out_offset=None, in_=ybuf,
                in_offset=bass.IndirectOffsetOnAxis(ap=slotsi_s[:, tile:tile + 1], axis=0)),
                sYG[cs], mark=("p5g", c))
        P.dma("sync", lambda q, c=c, cs=cs: q.dma_start(
            out=spld_s[cs][:],
            in_=x1sp[c * CH:(c + 1) * CH, :].rearrange("(a p) d -> p a d", p=128)),
            sYL[cs], mark=("p5l", c))
        if c >= 1:
            P.wtm("vector", ("p5o", c - 1))
        P.wtm("vector", ("p5g", c))
        P.wtm("vector", ("p5l", c))
        for a in range(TPC):
            tile = c * TPC + a
            P.do("vector", lambda q, tile=tile, a=a, cs=cs: q.scalar_tensor_tensor(
                out=z24_s[:, a, :], in0=y_s[cs][:, a, :],
                scalar=gate_s[:, tile:tile + 1], in1=spld_s[cs][:, a, :],
                op0=OP.mult, op1=OP.add, accum_out=z2p_s[:, a:a + 1]),
                mark=(("p5z", c) if a == TPC - 1 else None))
        for a in range(TPC):
            P.wtm("scalar", ("p5z", c))
            P.do("scalar", lambda q, a=a: q.activation(
                out=zsq_s[:], in_=z24_s[:, a, :], func=AF.Square,
                accum_out=z2q_s[:, a:a + 1]), mark=(("p5q", c) if a == TPC - 1 else None))
        P.wtm("vector", ("p5q", c))
        P.do("vector", lambda q: q.tensor_scalar(
            out=p5mu_s[:], in0=z2p_s[:], scalar1=1.0 / D, scalar2=None, op0=OP.mult))
        P.do("vector", lambda q: q.tensor_scalar(
            out=p5t_s[:], in0=z2q_s[:], scalar1=1.0 / D, scalar2=None, op0=OP.mult))
        P.do("vector", lambda q: q.tensor_tensor(
            out=p5v_s[:], in0=p5mu_s[:], in1=p5mu_s[:], op=OP.mult))
        P.do("vector", lambda q: q.tensor_tensor(
            out=p5t_s[:], in0=p5t_s[:], in1=p5v_s[:], op=OP.subtract),
            mark=("p5v", c))
        P.wtm("scalar", ("p5v", c))
        P.do("scalar", lambda q: q.activation(
            out=p5rs_s[:], in_=p5t_s[:], func=AF.Ln, bias=epscol_s[:, 0:1]))
        P.do("scalar", lambda q: q.activation(
            out=p5rs_s[:], in_=p5rs_s[:], func=AF.Exp, scale=-0.5), mark=("p5r", c))
        P.wtm("vector", ("p5r", c))
        P.do("vector", lambda q: q.tensor_tensor(
            out=p5nm_s[:], in0=p5mu_s[:], in1=p5rs_s[:], op=OP.mult))
        P.do("vector", lambda q: q.tensor_scalar(
            out=p5nm_s[:], in0=p5nm_s[:], scalar1=-1.0, scalar2=None, op0=OP.mult),
            mark=("p5c", c))
        oslot = c % 2
        if c >= 2:
            P.wtm("scalar", ("outw", c - 2))
        P.wtm("scalar", ("p5c", c))
        for a in range(TPC):
            P.do("scalar", lambda q, a=a, oslot=oslot: q.activation(
                out=o4_s[oslot][:, a, :], in_=z24_s[:, a, :], func=AF.Identity,
                scale=p5rs_s[:, a:a + 1], bias=p5nm_s[:, a:a + 1]),
                mark=(("p5o", c) if a == TPC - 1 else None))
        P.wtm("sync", ("p5o", c))
        P.dma("sync", lambda q, c=c, oslot=oslot: q.dma_start(
            out=out[c * CH:(c + 1) * CH, :].rearrange("(a p) d -> p a d", p=128),
            in_=o4_s[oslot][:]), sOut[oslot], mark=("outw", c))

    for c in range(NCH):
        p5_chunk(c)

    if debug:
        dbgr_s = bsb("dbgr_s", [128, NTILE, 4])
        for _s in sSP:
            P.wt("sync", _s, _s.n)
        P.dma("sync", lambda q: q.dma_start(out=dram["dbg_x1"], in_=x1sp), sOut[0])
        P.join("vector", [P.selfsem["vector"]])
        P.do("vector", lambda q: q.tensor_copy(dbgr_s[:, :, 0:1], eid_s[:].unsqueeze(2)))
        P.do("vector", lambda q: q.tensor_copy(dbgr_s[:, :, 1:2], lrank_s[:].unsqueeze(2)))
        P.do("vector", lambda q: q.tensor_copy(dbgr_s[:, :, 2:3], gate_s[:].unsqueeze(2)))
        P.do("vector", lambda q: q.tensor_copy(dbgr_s[:, :, 3:4], slots_s[:].unsqueeze(2)),
             mark="dbgr")
        P.wtm("sync", "dbgr")
        P.dma("sync", lambda q: q.dma_start(
            out=dram["dbg_route"].rearrange("(a p) f -> p a f", p=128),
            in_=dbgr_s[:]), sOut[0])
        P.dma("sync", lambda q: q.dma_start(out=dram["dbg_cnt"], in_=runb_s[:]), sOut[1])

    for _s in sOut:
        P.wt("sync", _s, _s.n)
    b5ctx.close()

    with octx:
        with nc.Block() as block:
            P.finalize(block)
    return nc


def _shard_inputs(inputs):
    x = np.asarray(inputs["L0_attn_x"], np.float32)
    Wk = np.asarray(inputs["L0_attn_Wk"], np.float32)
    Wv = np.asarray(inputs["L0_attn_Wv"], np.float32)
    Wr = np.asarray(inputs["L0_attn_Wr"], np.float32)
    Wo = np.ascontiguousarray(np.asarray(inputs["L0_attn_Wo"], np.float32))
    wrout = np.ascontiguousarray(np.asarray(inputs["L0_ffn_w_router"], np.float32))
    wgv = np.ascontiguousarray(np.asarray(inputs["L0_ffn_w_gate"], np.float32))
    wuv = np.ascontiguousarray(np.asarray(inputs["L0_ffn_w_up"], np.float32))
    wdv = np.ascontiguousarray(np.asarray(inputs["L0_ffn_w_down"], np.float32))
    pp = lambda v: np.ascontiguousarray(np.asarray(v, np.float32).reshape(G, 128).T)
    shared = {
        "wkT": np.ascontiguousarray(Wk.T), "wvT": np.ascontiguousarray(Wv.T),
        "wrT": np.ascontiguousarray(Wr.T), "woT": np.ascontiguousarray(Wo.T),
        "wo_n": Wo, "wrout": wrout, "wg": wgv, "wu": wuv, "wd": wdv,
        "tmk": pp(inputs["L0_attn_time_mix_k"]), "tmv": pp(inputs["L0_attn_time_mix_v"]),
        "tmr": pp(inputs["L0_attn_time_mix_r"]), "uu": pp(inputs["L0_attn_u"]),
        "ww": pp(inputs["L0_attn_w"]),
        "iotab": np.ascontiguousarray(
            np.broadcast_to(np.arange(E, dtype=np.float32), (128, E))),
        "identf": np.eye(128, dtype=np.float32),
        "trib": np.ascontiguousarray(np.triu(np.ones((128, 128), np.float32))),
        "rstmk": np.ascontiguousarray(np.broadcast_to(
            np.tile((np.arange(E) != 0).astype(np.float32), TPC), (128, TPC * E))),
    }
    in_maps = []
    for core in range(NCORE):
        xb = x[core * COREB:(core + 1) * COREB]
        m = dict(shared)
        m["xf"] = np.ascontiguousarray(xb.transpose(0, 2, 1))
        m["xt"] = np.ascontiguousarray(xb.reshape(TOK, D))
        in_maps.append(m)
    return in_maps


def kernel(**inputs):
    in_maps = _shard_inputs(inputs)
    nc = build_program(debug=False)
    res = run_bass_kernel_spmd(nc, in_maps, core_ids=list(range(NCORE)))
    outs = [np.asarray(r["out"], np.float32) for r in res.results]
    return np.concatenate(outs, axis=0).reshape(32, 2048, 1024)
